# version 06 (final)
# speedup vs baseline: 1.4223x; 1.0427x over previous
import numpy as np
import ml_dtypes
import concourse.bass as bass
import concourse.mybir as mybir
from concourse.bass_utils import run_bass_kernel_spmd

F32 = mybir.dt.float32
BF16 = mybir.dt.bfloat16
I32 = mybir.dt.int32
AF = mybir.ActivationFunctionType
ALU = mybir.AluOpType
AX = mybir.AxisListType

NCORES = 8
T = 4096
D = 1024



class PH:

    def __init__(self):
        self.phases = [[]]

    def set_sink(self, fn):
        self._sink = fn

    def ph(self):
        if self.phases[-1]:
            if getattr(self, "_sink", None) is not None and any(
                    eng == "tensor" and not dma for eng, fn, dma in self.phases[-1]):
                self.phases[-1].append(("tensor", self._sink, False))
            self.phases.append([])

    def add(self, eng, fn, dma=False):
        self.phases[-1].append((eng, fn, dma))

    def finalize(self, nc, block, sc, sd):
        phases = [p for p in self.phases if p]
        self.schedule = []
        c_tot, d_tot = 0, 0
        pre = []
        for p in phases:
            pre.append((c_tot, d_tot))
            engs_with_comp = set()
            for eng, fn, dma in p:
                if dma:
                    d_tot += 16
                else:
                    engs_with_comp.add(eng)
            c_tot += len(engs_with_comp)
        final_c, final_d = c_tot, d_tot

        def mk(engine_name):
            def body(e):
                last_cw, last_dw = -1, -1
                for pi, p in enumerate(phases):
                    mine = [(fn, dma) for eng, fn, dma in p if eng == engine_name]
                    if not mine:
                        continue
                    cw, dw = pre[pi]
                    if cw > last_cw and cw > 0:
                        e.wait_ge(sc, cw)
                        last_cw = cw
                    if dw > last_dw and dw > 0:
                        e.wait_ge(sd, dw)
                        last_dw = dw
                    comp = [fn for fn, dma in mine if not dma]
                    for fn, dma in mine:
                        inst = fn(e)
                        if dma:
                            inst.then_inc(sd, 16)
                        elif comp and fn is comp[-1]:
                            assert inst is not None, "inc-carrying item returned no instruction"
                            inst.then_inc(sc, 1)
                if engine_name == "sync":
                    e.wait_ge(sc, final_c)
                    e.wait_ge(sd, final_d)
            return body

        block.sync(mk("sync"))
        block.vector(mk("vector"))
        block.scalar(mk("scalar"))
        block.tensor(mk("tensor"))
        block.gpsimd(mk("gpsimd"))


def fin(e, SNK):
    return None


def mm_group(psum_ap, pairs, SNK):
    def fn(e):
        n = len(pairs)
        for i, (l, r) in enumerate(pairs):
            e.matmul(psum_ap, l, r, start=(i == 0), stop=(i == n - 1))
        return fin(e, SNK)
    return fn


def tr_group(items, idt, SNK):
    def fn(e):
        for o, i_ in items:
            e.transpose(o, i_, idt)
        return fin(e, SNK)
    return fn


def make_sink(nc, P, stack, pfx):
    snk = stack.enter_context(nc.psum_tensor(pfx + "snk", [128, 512], F32))
    z512 = stack.enter_context(nc.sbuf_tensor(pfx + "z512", [128, 512], BF16))
    P.add("gpsimd", lambda e: e.memset(z512[:], 0.0))
    P.set_sink(lambda e: e.matmul(snk[:, 0:512], z512[:, 0:128], z512[:, 0:512],
                                  start=True, stop=True))
    return (snk, z512)


def build_identity(P, nc, stack, pfx=""):
    idt = stack.enter_context(nc.sbuf_tensor(pfx + "idt", [128, 128], BF16))
    P.add("gpsimd", lambda e: e.memset(idt[:], 1.0))
    P.ph()
    P.add("gpsimd", lambda e: e.affine_select(
        idt[:], idt[:], [[-1, 128]], ALU.is_equal, 0.0, base=0, channel_multiplier=1))
    return idt



def emit_ln(P, nc, x_tm, lnx, sq_sc, st_sc, cst_eps, TC):
    c1024 = 1.0 / 1024.0
    for c in range(TC):
        P.add("vector", lambda e, c=c: e.reduce_sum(
            st_sc[:, c:c + 1], x_tm[:, c * 1024:(c + 1) * 1024], axis=AX.X))
        P.add("scalar", lambda e, c=c: e.activation(
            sq_sc[:], x_tm[:, c * 1024:(c + 1) * 1024], AF.Square,
            accum_out=st_sc[:, TC + c:TC + c + 1]))
    P.ph()
    P.add("vector", lambda e: e.tensor_scalar(
        st_sc[:, 2 * TC:3 * TC], st_sc[:, 0:TC], c1024, None, ALU.mult))
    P.ph()
    P.add("vector", lambda e: e.tensor_mul(
        st_sc[:, 3 * TC:4 * TC], st_sc[:, 2 * TC:3 * TC], st_sc[:, 2 * TC:3 * TC]))
    P.ph()
    P.add("vector", lambda e: e.scalar_tensor_tensor(
        st_sc[:, 4 * TC:5 * TC], st_sc[:, TC:2 * TC], c1024,
        st_sc[:, 3 * TC:4 * TC], ALU.mult, ALU.subtract))
    P.ph()
    P.add("scalar", lambda e: e.activation(
        st_sc[:, 5 * TC:6 * TC], st_sc[:, 4 * TC:5 * TC], AF.Sqrt, bias=cst_eps))
    P.ph()
    P.add("vector", lambda e: e.reciprocal(st_sc[:, 6 * TC:7 * TC], st_sc[:, 5 * TC:6 * TC]))
    P.ph()
    P.add("vector", lambda e: e.tensor_mul(
        st_sc[:, 7 * TC:8 * TC], st_sc[:, 2 * TC:3 * TC], st_sc[:, 6 * TC:7 * TC]))
    P.ph()
    for c in range(TC):
        P.add("vector", lambda e, c=c: e.tensor_scalar(
            lnx[:, c * 1024:(c + 1) * 1024], x_tm[:, c * 1024:(c + 1) * 1024],
            st_sc[:, 6 * TC + c:6 * TC + c + 1], st_sc[:, 7 * TC + c:7 * TC + c + 1],
            ALU.mult, ALU.subtract))
    P.ph()



MAMBA_INS = ["wxt", "cw", "cb", "dtb", "dpar", "xpt", "dwt", "wot"]


def declare_mamba_ins(nc):
    return dict(
        wxt=nc.dram_tensor("wxt", [1024, 4096], F32, kind="ExternalInput").ap(),
        cw=nc.dram_tensor("cw", [128, 64], F32, kind="ExternalInput").ap(),
        cb=nc.dram_tensor("cb", [128, 16], F32, kind="ExternalInput").ap(),
        dtb=nc.dram_tensor("dtb", [128, 16], F32, kind="ExternalInput").ap(),
        dpar=nc.dram_tensor("dpar", [128, 16], F32, kind="ExternalInput").ap(),
        xpt=nc.dram_tensor("xpt", [128, 256], F32, kind="ExternalInput").ap(),
        dwt=nc.dram_tensor("dwt", [16, 2048], F32, kind="ExternalInput").ap(),
        wot=nc.dram_tensor("wot", [2048, 1024], F32, kind="ExternalInput").ap())


def emit_mamba(nc, P, stack, pfx, ins, x_in, x_out):
    x = x_in
    xo = x_out
    wxt, cw, cb, dtb, dpar, xpt, dwt, wot = (ins[k] for k in MAMBA_INS)

    NB = 16
    B = 256
    TC = B // 128

    if True:
        sb = lambda name, shape, dt=F32: stack.enter_context(nc.sbuf_tensor(pfx + name, shape, dt))
        psum = lambda name, shape, dt=F32: stack.enter_context(nc.psum_tensor(pfx + name, shape, dt))

        wxt_sb = sb("wxt_sb", [128, 8 * 4096], BF16)
        wot_sb = sb("wot_sb", [128, 16 * 1024], BF16)
        dwt_sb = sb("dwt_sb", [16, 2048], BF16)
        cw_sb = sb("cw_sb", [128, 64])
        cb_sb = sb("cb_sb", [128, 16])
        dtb_sb = sb("dtb_sb", [128, 16])
        dpar_sb = sb("dpar_sb", [128, 16])
        xpt_sb = sb("xpt_sb", [128, 256])
        ones_col = sb("ones_col", [1, 128])
        ones128 = sb("ones128", [128, 1])
        cst = sb("cst", [128, 2])
        zb = sb("zb", [128, 128], BF16)
        zf = sb("zf", [128, 128])

        x_tm = sb("x_tm", [128, TC * 1024])
        sq_sc = sb("sq_sc", [128, 1024])
        st_sc = sb("st_sc", [128, 8 * TC])
        lnx = sb("lnx", [128, TC * 1024], BF16)
        lnxT = sb("lnxT", [128, 8 * B], BF16)
        xi = sb("xi", [128, 16 * (B + 3)], BF16)
        xc = sb("xc", [128, 16 * B])
        ybf = sb("ybf", [128, 16 * B], BF16)
        bssm = sb("bssm", [16, B], BF16)
        dt_sc = sb("dt_sc", [128, 8 * B])
        e_sc = sb("e_sc", [128, 4 * B])
        alpha_sb = sb("alpha_sb", [1, B])
        a_sb = sb("a_sb", [128, B])
        b_all = sb("b_all", [128, 16 * B])
        h_all = sb("h_all", [128, 16 * B])
        sz_sc = sb("sz_sc", [128, 8 * B], BF16)
        hcar = sb("hcar", [128, 16])

        pXI0 = psum("pXI0", [128, 1024])
        pXI1 = psum("pXI1", [128, 1024])
        pTR = psum("pTR", [128, 2048], BF16)
        pSM = psum("pSM", [128, 512])

        idt = build_identity(P, nc, stack, pfx)
        SNK = make_sink(nc, P, stack, pfx)
        for dc in range(8):
            P.add("gpsimd", lambda e, dc=dc: e.dma_start(
                out=wxt_sb[:, dc * 4096:(dc + 1) * 4096], in_=wxt[dc * 128:(dc + 1) * 128, :]), dma=True)
        for s in range(16):
            P.add("gpsimd", lambda e, s=s: e.dma_start(
                out=wot_sb[:, s * 1024:(s + 1) * 1024], in_=wot[s * 128:(s + 1) * 128, :]), dma=True)
        P.add("gpsimd", lambda e: e.dma_start(out=dwt_sb[:], in_=dwt), dma=True)
        for name, dst, src in [("cw", cw_sb, cw), ("cb", cb_sb, cb), ("dtb", dtb_sb, dtb),
                               ("dpar", dpar_sb, dpar), ("xpt", xpt_sb, xpt)]:
            P.add("sync", lambda e, dst=dst, src=src: e.dma_start(out=dst[:], in_=src), dma=True)
        P.add("vector", lambda e: e.memset(ones_col[:], 1.0))
        P.add("vector", lambda e: e.memset(ones128[:], 1.0))
        P.add("vector", lambda e: e.memset(hcar[:], 0.0))
        P.add("vector", lambda e: e.memset(xi[:], 0.0))
        P.add("gpsimd", lambda e: e.memset(zb[:], 0.0))
        P.add("gpsimd", lambda e: e.memset(zf[:], 0.0))
        P.add("vector", lambda e: e.memset(cst[:, 0:1], 1e-5))
        P.add("vector", lambda e: e.memset(cst[:, 1:2], 1.0 / 2048.0))
        P.ph()

        c1024 = 1.0 / 1024.0

        for b in range(NB):
            t0 = b * B
            P.add("sync", lambda e, t0=t0: e.dma_start(
                out=x_tm[:].rearrange("p (c d) -> p c d", c=TC),
                in_=x[t0:t0 + B, :].rearrange("(c p) d -> p c d", p=128)), dma=True)
            P.ph()
            emit_ln(P, nc, x_tm, lnx, sq_sc, st_sc, cst[:, 0:1], TC)
            items = [(pTR[:, (c * 8 + dc) * 128:(c * 8 + dc + 1) * 128],
                      lnx[:, c * 1024 + dc * 128:c * 1024 + (dc + 1) * 128])
                     for c in range(TC) for dc in range(8)]
            P.add("tensor", tr_group(items, idt[:], SNK))
            P.ph()
            def trc_fn(e):
                last = None
                for c in range(TC):
                    for dc in range(8):
                        last = e.tensor_copy(lnxT[:, dc * B + c * 128:dc * B + (c + 1) * 128],
                                             pTR[:, (c * 8 + dc) * 128:(c * 8 + dc + 1) * 128])
                return last
            P.add("vector", trc_fn)
            P.ph()

            NG = 4
            pXIs = [pXI0, pXI1]
            for g in range(NG + 1):
                if g < NG:
                    pg = pXIs[g % 2]
                    for j in range(4):
                        ec = g * 4 + j
                        pairs = [(wxt_sb[:, dc * 4096 + ec * 128:dc * 4096 + (ec + 1) * 128],
                                  lnxT[:, dc * B:(dc + 1) * B]) for dc in range(8)]
                        P.add("tensor", mm_group(pg[:, j * B:(j + 1) * B], pairs, SNK))
                if g > 0:
                    pg = pXIs[(g - 1) % 2]
                    def ev_fn(e, g=g, pg=pg):
                        last = None
                        for j in range(4):
                            ec = (g - 1) * 4 + j
                            last = e.copy(xi[:, ec * (B + 3) + 3:ec * (B + 3) + 3 + B],
                                          pg[:, j * B:(j + 1) * B])
                        return last
                    P.add("scalar", ev_fn)
                P.ph()

            def conv_fn(e):
                last = None
                for s in range(16):
                    o = s * (B + 3)
                    xcs = xc[:, s * B:(s + 1) * B]
                    last = e.tensor_scalar(xcs, xi[:, o:o + B], cw_sb[:, s * 4:s * 4 + 1], None, ALU.mult)
                    for k in range(1, 4):
                        last = e.scalar_tensor_tensor(xcs, xi[:, o + k:o + k + B],
                                                      cw_sb[:, s * 4 + k:s * 4 + k + 1], xcs,
                                                      ALU.mult, ALU.add)
                return last
            P.add("vector", conv_fn)
            P.ph()
            def silu_fn(e):
                last = None
                for s in range(16):
                    last = e.activation(xc[:, s * B:(s + 1) * B], xc[:, s * B:(s + 1) * B],
                                        AF.Silu, bias=cb_sb[:, s:s + 1])
                return last
            P.add("scalar", silu_fn)
            def halo_fn(e):
                last = None
                for s in range(16):
                    o = s * (B + 3)
                    last = e.tensor_copy(xi[:, o:o + 3], xi[:, o + B:o + B + 3])
                return last
            P.add("gpsimd", halo_fn)
            P.ph()

            pairs = [(xpt_sb[:, s * 16:(s + 1) * 16], xc[:, s * B:(s + 1) * B]) for s in range(16)]
            P.add("tensor", mm_group(pSM[0:16, 0:B], pairs, SNK))
            P.ph()
            P.add("scalar", lambda e: e.copy(bssm[:], pSM[0:16, 0:B]))
            P.ph()

            for g in range(6):
                if g < 4:
                    pg = pXIs[g % 2]
                    def dt_mm(e, g=g, pg=pg):
                        for j in range(4):
                            s = g * 4 + j
                            e.matmul(pg[:, j * B:(j + 1) * B],
                                     dwt_sb[:, s * 128:(s + 1) * 128], bssm[:],
                                     start=True, stop=False)
                        return fin(e, SNK)
                    P.add("tensor", dt_mm)
                if 1 <= g <= 4:
                    pg = pXIs[(g - 1) % 2]
                    half = ((g - 1) % 2) * 4 * B
                    def dt_act(e, g=g, pg=pg, half=half):
                        last = None
                        for j in range(4):
                            s = (g - 1) * 4 + j
                            last = e.activation(e_sc[:, j * B:(j + 1) * B], pg[:, j * B:(j + 1) * B],
                                                AF.Exp, bias=dtb_sb[:, s:s + 1])
                        for j in range(4):
                            last = e.activation(dt_sc[:, half + j * B:half + (j + 1) * B],
                                                e_sc[:, j * B:(j + 1) * B], AF.Ln, bias=1.0)
                        return last
                    P.add("scalar", dt_act)
                if g >= 2:
                    half = ((g - 2) % 2) * 4 * B
                    def al_mm(e, g=g, half=half):
                        for j in range(4):
                            s = (g - 2) * 4 + j
                            e.matmul(pSM[0:1, B:2 * B], ones128[:],
                                     dt_sc[:, half + j * B:half + (j + 1) * B],
                                     start=(s == 0), stop=False)
                        return fin(e, SNK)
                    P.add("tensor", al_mm)
                P.ph()
            P.add("scalar", lambda e: e.activation(alpha_sb[:], pSM[0:1, B:2 * B], AF.Sigmoid, scale=cst[0:1, 1:2]))
            P.ph()
            def bcast_mm(e):
                e.matmul(pSM[:, 0:B], ones_col[:], alpha_sb[:], start=True, stop=False)
                return fin(e, SNK)
            P.add("tensor", bcast_mm)
            P.ph()
            P.add("vector", lambda e: e.tensor_copy(a_sb[:], pSM[:, 0:B]))
            P.ph()

            def bprep_fn(e):
                last = None
                for s in range(16):
                    xcs = xc[:, s * B:(s + 1) * B]
                    last = e.tensor_mul(b_all[:, s * B:(s + 1) * B], a_sb[:], xcs)
                    last = e.tensor_sub(b_all[:, s * B:(s + 1) * B], xcs, b_all[:, s * B:(s + 1) * B])
                return last
            P.add("vector", bprep_fn)
            P.ph()
            def scan_fn(e):
                last = None
                for s in range(16):
                    last = e.tensor_tensor_scan(h_all[:, s * B:(s + 1) * B], a_sb[:],
                                                b_all[:, s * B:(s + 1) * B], hcar[:, s:s + 1],
                                                ALU.mult, ALU.add)
                return last
            P.add("vector", scan_fn)
            P.ph()
            def post_fn(e):
                last = None
                for s in range(16):
                    last = e.tensor_copy(hcar[:, s:s + 1], h_all[:, s * B + B - 1:s * B + B])
                for s in range(16):
                    xcs = xc[:, s * B:(s + 1) * B]
                    last = e.scalar_tensor_tensor(xcs, h_all[:, s * B:(s + 1) * B],
                                                  dpar_sb[:, s:s + 1], xcs, ALU.mult, ALU.add)
                return last
            P.add("vector", post_fn)
            P.ph()

            for g in range(6):
                if g < 4:
                    pg = pXIs[g % 2]
                    for j in range(4):
                        ec = g * 4 + j
                        pairs = [(wxt_sb[:, dc * 4096 + 2048 + ec * 128:dc * 4096 + 2048 + (ec + 1) * 128],
                                  lnxT[:, dc * B:(dc + 1) * B]) for dc in range(8)]
                        P.add("tensor", mm_group(pg[:, j * B:(j + 1) * B], pairs, SNK))
                if 1 <= g <= 4:
                    pg = pXIs[(g - 1) % 2]
                    half = ((g - 1) % 2) * 4 * B
                    def z_silu(e, g=g, pg=pg, half=half):
                        last = None
                        for j in range(4):
                            last = e.activation(sz_sc[:, half + j * B:half + (j + 1) * B],
                                                pg[:, j * B:(j + 1) * B], AF.Silu)
                        return last
                    P.add("scalar", z_silu)
                if g >= 2:
                    half = ((g - 2) % 2) * 4 * B
                    def z_gate(e, g=g, half=half):
                        last = None
                        for j in range(4):
                            s = (g - 2) * 4 + j
                            last = e.tensor_mul(ybf[:, s * B:(s + 1) * B], xc[:, s * B:(s + 1) * B],
                                                sz_sc[:, half + j * B:half + (j + 1) * B])
                        return last
                    P.add("vector", z_gate)
                P.ph()

            for c in range(TC):
                pg = pXIs[c % 2]
                for h in range(2):
                    pairs = [(ybf[:, s * B + c * 128:s * B + (c + 1) * 128],
                              wot_sb[:, s * 1024 + h * 512:s * 1024 + (h + 1) * 512]) for s in range(16)]
                    P.add("tensor", mm_group(pg[:, h * 512:(h + 1) * 512], pairs, SNK))
                P.ph()
                P.add("vector", lambda e, c=c, pg=pg: e.tensor_add(
                    x_tm[:, c * 1024:(c + 1) * 1024], pg[:, 0:1024], x_tm[:, c * 1024:(c + 1) * 1024]))
                P.ph()
            P.add("sync", lambda e, t0=t0: e.dma_start(
                out=xo[t0:t0 + B, :].rearrange("(c p) d -> p c d", p=128),
                in_=x_tm[:].rearrange("p (c d) -> p c d", c=TC)), dma=True)
            P.ph()


def build_mamba():
    import contextlib
    nc = bass.Bass("TRN2", target_bir_lowering=False, debug=False)
    x = nc.dram_tensor("x", [T, D], F32, kind="ExternalInput").ap()
    ins = declare_mamba_ins(nc)
    xo = nc.dram_tensor("xo", [T, D], F32, kind="ExternalOutput").ap()
    with contextlib.ExitStack() as stack:
        sc = stack.enter_context(nc.semaphore("sc"))
        sd = stack.enter_context(nc.semaphore("sd"))
        P = PH()
        with contextlib.ExitStack() as inner:
            emit_mamba(nc, P, inner, "", ins, x, xo)
        with nc.Block() as block:
            P.finalize(nc, block, sc, sd)
    return nc


def mamba_host_inputs(inputs):
    x = np.ascontiguousarray(inputs["L0_attn_a_x"].astype(np.float32))
    wxt = np.ascontiguousarray(inputs["L0_attn_a_in_proj_w"].T)
    cw = np.ascontiguousarray(inputs["L0_attn_a_conv_w"].reshape(2048, 4).reshape(16, 128, 4)
                              .transpose(1, 0, 2).reshape(128, 64))
    cb = np.ascontiguousarray(inputs["L0_attn_a_conv_b"].reshape(16, 128).T)
    dtb = np.ascontiguousarray(inputs["L0_attn_a_dt_proj_b"].reshape(16, 128).T)
    dpar = np.ascontiguousarray(inputs["L0_attn_a_D_param"].reshape(16, 128).T)
    xpt = np.ascontiguousarray(inputs["L0_attn_a_x_proj_w"][0:16].T.reshape(16, 128, 16)
                               .transpose(1, 0, 2).reshape(128, 256))
    dwt = np.ascontiguousarray(inputs["L0_attn_a_dt_proj_w"].T)
    wot = np.ascontiguousarray(inputs["L0_attn_a_out_proj_w"].T)
    shared = dict(wxt=wxt, cw=cw, cb=cb, dtb=dtb, dpar=dpar, xpt=xpt, dwt=dwt, wot=wot)
    return [dict(shared, x=np.ascontiguousarray(x[c])) for c in range(NCORES)]






def build_moe0(dbg=False):
    nc = bass.Bass("TRN2", target_bir_lowering=False, debug=False)
    TB = 512
    TC = TB // 128
    xb = nc.dram_tensor("xb", [TB, D], F32, kind="ExternalInput").ap()
    w1 = nc.dram_tensor("w1", [1024, 8192], F32, kind="ExternalInput").ap()
    w2 = nc.dram_tensor("w2", [8192, 1024], F32, kind="ExternalInput").ap()
    ew = nc.dram_tensor("ew", [128, TC * 2], F32, kind="ExternalInput").ap()
    xbo = nc.dram_tensor("xbo", [TB, D], F32, kind="ExternalOutput").ap()
    if dbg:
        d_lnxT = nc.dram_tensor("d_lnxT", [128, 8 * TB], F32, kind="ExternalOutput").ap()
        d_h = nc.dram_tensor("d_h", [128, 8 * TB], F32, kind="ExternalOutput").ap()
        d_acc = nc.dram_tensor("d_acc", [128, TC * 1024], F32, kind="ExternalOutput").ap()
        d_lnx = nc.dram_tensor("d_lnx", [128, TC * 1024], F32, kind="ExternalOutput").ap()
        d_st = nc.dram_tensor("d_st", [128, 32], F32, kind="ExternalOutput").ap()

    import contextlib
    with contextlib.ExitStack() as stack:
        sc = stack.enter_context(nc.semaphore("sc"))
        sd = stack.enter_context(nc.semaphore("sd"))
        sb = lambda name, shape, dt=F32: stack.enter_context(nc.sbuf_tensor(name, shape, dt))
        psum = lambda name, shape, dt=F32: stack.enter_context(nc.psum_tensor(name, shape, dt))

        w1g = [sb("w1g0", [128, 8 * 1024], BF16), sb("w1g1", [128, 8 * 1024], BF16)]
        w2g = [sb("w2g0", [128, 8 * 1024], BF16), sb("w2g1", [128, 8 * 1024], BF16)]
        x_tm = sb("x_tm", [128, TC * 1024])
        sq_sc = sb("sq_sc", [128, 1024])
        st_sc = sb("st_sc", [128, 8 * TC])
        lnx = sb("lnx", [128, TC * 1024], BF16)
        lnxT = sb("lnxT", [128, 8 * TB], BF16)
        h_bf = sb("h_bf", [128, 8 * TB], BF16)
        acc = sb("acc", [128, TC * 1024])
        ew_sb = sb("ew_sb", [128, TC * 2])
        s_sb = sb("s_sb", [128, TC])
        cst = sb("cst", [128, 1])
        zb = sb("zbm", [128, 128], BF16)
        pH = psum("pH", [128, 1024])
        pY0 = psum("pY0", [128, 1024])
        pY1 = psum("pY1", [128, 1024])
        pTR = psum("pTR", [128, 1024], BF16)

        P = PH()
        idt = build_identity(P, nc, stack)
        SNK = make_sink(nc, P, stack, "")
        P.add("sync", lambda e: e.dma_start(
            out=x_tm[:].rearrange("p (c d) -> p c d", c=TC),
            in_=xb.rearrange("(c p) d -> p c d", p=128)), dma=True)
        P.add("sync", lambda e: e.dma_start(out=ew_sb[:], in_=ew), dma=True)
        P.add("vector", lambda e: e.memset(cst[:], 1e-5))
        P.add("gpsimd", lambda e: e.dma_start(
            out=w1g[0][:].rearrange("p (dc i) -> p dc i", dc=8),
            in_=w1[:, 0:1024].rearrange("(dc p) i -> p dc i", p=128)), dma=True)
        P.add("gpsimd", lambda e: e.dma_start(
            out=w2g[0][:].rearrange("p (ic f) -> p ic f", ic=8),
            in_=w2[0:1024, :].rearrange("(ic p) f -> p ic f", p=128)), dma=True)
        P.ph()

        P.add("vector", lambda e: e.tensor_add(s_sb[:], ew_sb[:, 0:TC], ew_sb[:, TC:2 * TC]))
        P.add("gpsimd", lambda e: e.memset(acc[:], 0.0))
        emit_ln(P, nc, x_tm, lnx, sq_sc, st_sc, cst[:, 0:1], TC)
        for c in range(4):
            items = [(pTR[:, dc * 128:(dc + 1) * 128],
                      lnx[:, c * 1024 + dc * 128:c * 1024 + (dc + 1) * 128])
                     for dc in range(8)]
            P.add("tensor", tr_group(items, idt[:], SNK))
            P.ph()
            def trc_fn(e, c=c):
                last = None
                for dc in range(8):
                    last = e.tensor_copy(lnxT[:, dc * TB + c * 128:dc * TB + (c + 1) * 128],
                                         pTR[:, dc * 128:(dc + 1) * 128])
                return last
            P.add("vector", trc_fn)
            P.ph()

        pYs = [pY0, pY1]
        for g in range(8):
            gb = g % 2
            if g < 7:
                P.add("gpsimd", lambda e, g=g: e.dma_start(
                    out=w1g[(g + 1) % 2][:].rearrange("p (dc i) -> p dc i", dc=8),
                    in_=w1[:, (g + 1) * 1024:(g + 2) * 1024].rearrange("(dc p) i -> p dc i", p=128)), dma=True)
                P.add("gpsimd", lambda e, g=g: e.dma_start(
                    out=w2g[(g + 1) % 2][:].rearrange("p (ic f) -> p ic f", ic=8),
                    in_=w2[(g + 1) * 1024:(g + 2) * 1024, :].rearrange("(ic p) f -> p ic f", p=128)), dma=True)
            for ic in range(9):
                if ic < 8:
                    pairs = [(w1g[gb][:, dc * 1024 + ic * 128:dc * 1024 + (ic + 1) * 128],
                              lnxT[:, dc * TB:(dc + 1) * TB]) for dc in range(8)]
                    P.add("tensor", mm_group(pH[:, (ic % 2) * 512:(ic % 2 + 1) * 512], pairs, SNK))
                if ic > 0:
                    P.add("scalar", lambda e, ic=ic, gb=gb: e.activation(
                        h_bf[:, (ic - 1) * TB:ic * TB], pH[:, ((ic - 1) % 2) * 512:((ic - 1) % 2 + 1) * 512],
                        AF.Gelu_apprx_tanh))
                P.ph()
            for c in range(TC + 1):
                if c < TC:
                    py = pYs[c % 2]
                    for hh in range(2):
                        pairs = [(h_bf[:, ic * TB + c * 128:ic * TB + (c + 1) * 128],
                                  w2g[gb][:, ic * 1024 + hh * 512:ic * 1024 + (hh + 1) * 512])
                                 for ic in range(8)]
                        P.add("tensor", mm_group(py[:, hh * 512:(hh + 1) * 512], pairs, SNK))
                if c > 0:
                    py = pYs[(c - 1) % 2]
                    P.add("vector", lambda e, c=c, py=py: e.tensor_add(
                        acc[:, (c - 1) * 1024:c * 1024], py[:, 0:1024], acc[:, (c - 1) * 1024:c * 1024]))
                P.ph()

        if dbg:
            dcp = sb("dcp", [128, 8 * TB])
            P.add("vector", lambda e: e.tensor_copy(dcp[:, 0:TC * 1024], lnx[:]))
            P.ph()
            P.add("sync", lambda e: e.dma_start(out=d_lnx, in_=dcp[:, 0:TC * 1024]), dma=True)
            P.add("sync", lambda e: e.dma_start(out=d_st, in_=st_sc[:]), dma=True)
            P.ph()
            P.add("vector", lambda e: e.tensor_copy(dcp[:], lnxT[:]))
            P.ph()
            P.add("sync", lambda e: e.dma_start(out=d_lnxT, in_=dcp[:]), dma=True)
            P.ph()
            P.add("vector", lambda e: e.tensor_copy(dcp[:], h_bf[:]))
            P.ph()
            P.add("sync", lambda e: e.dma_start(out=d_h, in_=dcp[:]), dma=True)
            P.add("sync", lambda e: e.dma_start(out=d_acc, in_=acc[:]), dma=True)
            P.ph()
        for c in range(TC):
            P.add("vector", lambda e, c=c: e.scalar_tensor_tensor(
                x_tm[:, c * 1024:(c + 1) * 1024], acc[:, c * 1024:(c + 1) * 1024],
                s_sb[:, c:c + 1], x_tm[:, c * 1024:(c + 1) * 1024], ALU.mult, ALU.add))
        P.ph()
        P.add("sync", lambda e: e.dma_start(
            out=xbo.rearrange("(c p) d -> p c d", p=128),
            in_=x_tm[:].rearrange("p (c d) -> p c d", c=TC)), dma=True)
        P.ph()

        with nc.Block() as block:
            P.finalize(nc, block, sc, sd)
    return nc


def moe0_host_inputs(inputs, x_full):
    xb0 = np.ascontiguousarray(x_full[0])
    w1 = np.ascontiguousarray(inputs["L0_ffn_w1"].astype(np.float32))
    w2 = np.ascontiguousarray(inputs["L0_ffn_w2"].astype(np.float32))
    ew = np.asarray(inputs["L0_ffn_expert_weights"], np.float32)
    maps = []
    for c in range(NCORES):
        rows = slice(c * 512, (c + 1) * 512)
        ew_c = ew[rows].reshape(4, 128, 2).transpose(1, 2, 0).reshape(128, 8)
        maps.append(dict(xb=np.ascontiguousarray(xb0[rows]), w1=w1, w2=w2,
                         ew=np.ascontiguousarray(ew_c)))
    return maps



ATTN0_INS = ["wqt0", "wkt0", "wvt0", "bq0", "bk0", "bv0", "cosT", "sinT"]


def declare_attn0_ins(nc):
    return dict(
        wqt0=nc.dram_tensor("wqt0", [1024, 1024], F32, kind="ExternalInput").ap(),
        wkt0=nc.dram_tensor("wkt0", [1024, 1024], F32, kind="ExternalInput").ap(),
        wvt0=nc.dram_tensor("wvt0", [1024, 1024], F32, kind="ExternalInput").ap(),
        bq0=nc.dram_tensor("bq0", [128, 8], F32, kind="ExternalInput").ap(),
        bk0=nc.dram_tensor("bk0", [128, 8], F32, kind="ExternalInput").ap(),
        bv0=nc.dram_tensor("bv0", [1, 1024], F32, kind="ExternalInput").ap(),
        cosT=nc.dram_tensor("cosT", [512, T], F32, kind="ExternalInput").ap(),
        sinT=nc.dram_tensor("sinT", [512, T], F32, kind="ExternalInput").ap())


def emit_attn0(nc, P, stack, pfx, ins, x_in, x_out, dbg=False, nbq=8, do_k=True, do_v=True, do_attn=True):
    x = x_in
    xo = x_out
    wqt, wkt, wvt, bq, bk, bv, cosT, sinT = (ins[k] for k in ATTN0_INS)
    qdr = nc.dram_tensor(pfx + "qdr", [1024, T], BF16, kind="Internal").ap()
    lndr = nc.dram_tensor(pfx + "lndr", [1024, T], BF16, kind="Internal").ap()
    if dbg:
        d_k = nc.dram_tensor("d_k", [128, T], F32, kind="ExternalOutput").ap()
        d_v = nc.dram_tensor("d_v", [128, 1024], F32, kind="ExternalOutput").ap()
        d_q = nc.dram_tensor("d_q", [128, 512], F32, kind="ExternalOutput").ap()
        d_probs = nc.dram_tensor("d_probs", [128, T], F32, kind="ExternalOutput").ap()
        d_rs = nc.dram_tensor("d_rs", [128, 8], F32, kind="ExternalOutput").ap()
        d_ct = nc.dram_tensor("d_ct", [128, 512], F32, kind="ExternalOutput").ap()
        d_qraw = nc.dram_tensor("d_qraw", [128, 512], F32, kind="ExternalOutput").ap()
        d_w = nc.dram_tensor("d_w", [128, 1024], F32, kind="ExternalOutput").ap()
        d_ln0 = nc.dram_tensor("d_ln0", [128, 512], F32, kind="ExternalOutput").ap()

    NBQ = 8
    B = 512

    import contextlib
    if True:
        sb = lambda name, shape, dt=F32: stack.enter_context(nc.sbuf_tensor(pfx + name, shape, dt))
        psum = lambda name, shape, dt=F32: stack.enter_context(nc.psum_tensor(pfx + name, shape, dt))

        k_sb = sb("k_sb", [128, 8 * T], BF16)
        v_sb = sb("v_sb", [128, 32 * 1024], BF16)
        cst = sb("cst", [128, 2])
        zb = sb("zba", [128, 128], BF16)
        ones_col = sb("ones_col", [1, 128])
        pA = psum("pA", [128, 1024])
        pB = psum("pB", [128, 1024])
        pPV = psum("pPV", [128, 1024])
        pTR = psum("pTR", [128, 1024], BF16)

        idt = build_identity(P, nc, stack, pfx)
        SNK = make_sink(nc, P, stack, pfx)
        P.add("vector", lambda e: e.memset(cst[:, 0:1], 1e-5))
        P.add("vector", lambda e: e.memset(cst[:, 1:2], 1.0 / 32.0))
        P.add("vector", lambda e: e.memset(ones_col[:], 1.0))
        P.add("gpsimd", lambda e: e.memset(zb[:], 0.0))

        with contextlib.ExitStack() as s1:
            sb1 = lambda name, shape, dt=F32: s1.enter_context(nc.sbuf_tensor(pfx + name, shape, dt))
            wq_sb = sb1("wq_sb", [128, 8 * 1024], BF16)
            wk_sb = sb1("wk_sb", [128, 8 * 1024], BF16)
            bq_sb = sb1("bq_sb", [128, 8])
            bk_sb = sb1("bk_sb", [128, 8])
            x_tm = sb1("x_tm", [128, 2 * 1024])
            sq_sc = sb1("sq_sc", [128, 1024], BF16)
            st_sc = sb1("st_sc", [128, 16])
            lnx = sb1("lnx", [128, 4 * 1024], BF16)
            lnxT = sb1("lnxT", [128, 8 * B], BF16)
            ct_sb = sb1("ct_sb", [128, 2 * 512], BF16)
            stb_sb = sb1("stb_sb", [128, 2 * 512], BF16)
            r0 = sb1("r0", [128, B], BF16); r1 = sb1("r1", [128, B], BF16)
            m0 = sb1("m0", [128, B], BF16); m1 = sb1("m1", [128, B], BF16)
            q2 = sb1("q2", [128, 4 * 512], BF16)

            for dc in range(8):
                P.add("gpsimd", lambda e, dc=dc: e.dma_start(
                    out=wq_sb[:, dc * 1024:(dc + 1) * 1024], in_=wqt[dc * 128:(dc + 1) * 128, :]), dma=True)
                P.add("gpsimd", lambda e, dc=dc: e.dma_start(
                    out=wk_sb[:, dc * 1024:(dc + 1) * 1024], in_=wkt[dc * 128:(dc + 1) * 128, :]), dma=True)
            P.add("sync", lambda e: e.dma_start(out=bq_sb[:], in_=bq), dma=True)
            P.add("sync", lambda e: e.dma_start(out=bk_sb[:], in_=bk), dma=True)
            P.ph()

            for b in range(nbq):
                t0 = b * B
                for hf in range(2):
                    P.add("sync", lambda e, t0=t0, hf=hf: e.dma_start(
                        out=x_tm[:].rearrange("p (c d) -> p c d", c=2),
                        in_=x[t0 + hf * 256:t0 + (hf + 1) * 256, :].rearrange("(c p) d -> p c d", p=128)), dma=True)
                    P.ph()
                    emit_ln(P, nc, x_tm, lnx[:, hf * 2048:(hf + 1) * 2048], sq_sc, st_sc, cst[:, 0:1], 2)
                for c in range(4):
                    items = [(pTR[:, dc * 128:(dc + 1) * 128],
                              lnx[:, c * 1024 + dc * 128:c * 1024 + (dc + 1) * 128])
                             for dc in range(8)]
                    P.add("tensor", tr_group(items, idt[:], SNK))
                    P.ph()
                    def trc_fn(e, c=c):
                        last = None
                        for dc in range(8):
                            last = e.tensor_copy(lnxT[:, dc * B + c * 128:dc * B + (c + 1) * 128],
                                                 pTR[:, dc * 128:(dc + 1) * 128])
                        return last
                    P.add("vector", trc_fn)
                    P.ph()
                for dc in range(8):
                    P.add("sync", lambda e, dc=dc, t0=t0: e.dma_start(
                        out=lndr[dc * 128:(dc + 1) * 128, t0:t0 + B],
                        in_=lnxT[:, dc * B:(dc + 1) * B]), dma=True)

                for (w_sb, b_sb, is_q) in (((wq_sb, bq_sb, True), (wk_sb, bk_sb, False)) if do_k else ((wq_sb, bq_sb, True),)):
                    pXs = [pA, pB]
                    for g in range(6):
                        if g < 4:
                            pg = pXs[g % 2]
                            par = g % 2
                            P.add("gpsimd", lambda e, g=g, par=par, t0=t0: e.dma_start(
                                out=ct_sb[:, par * 512:(par + 1) * 512],
                                in_=cosT[g * 128:(g + 1) * 128, t0:t0 + B]), dma=True)
                            P.add("gpsimd", lambda e, g=g, par=par, t0=t0: e.dma_start(
                                out=stb_sb[:, par * 512:(par + 1) * 512],
                                in_=sinT[g * 128:(g + 1) * 128, t0:t0 + B]), dma=True)
                            pairs0 = [(w_sb[:, dc * 1024 + g * 128:dc * 1024 + (g + 1) * 128],
                                       lnxT[:, dc * B:(dc + 1) * B]) for dc in range(8)]
                            pairs1 = [(w_sb[:, dc * 1024 + (g + 4) * 128:dc * 1024 + (g + 5) * 128],
                                       lnxT[:, dc * B:(dc + 1) * B]) for dc in range(8)]
                            P.add("tensor", mm_group(pg[:, 0:B], pairs0, SNK))
                            P.add("tensor", mm_group(pg[:, B:2 * B], pairs1, SNK))
                        if 1 <= g <= 4:
                            c = g - 1
                            par = (g - 1) % 2
                            pg = pXs[(g - 1) % 2]
                            if is_q:
                                o_lo = q2[:, par * 1024:par * 1024 + 512]
                                o_hi = q2[:, par * 1024 + 512:par * 1024 + 1024]
                            else:
                                o_lo = k_sb[:, c * T + t0:c * T + t0 + B]
                                o_hi = k_sb[:, (c + 4) * T + t0:(c + 4) * T + t0 + B]
                            def rope_fn(e, c=c, par=par, pg=pg, b_sb=b_sb, o_lo=o_lo, o_hi=o_hi):
                                cts = ct_sb[:, par * 512:(par + 1) * 512]
                                sts = stb_sb[:, par * 512:(par + 1) * 512]
                                e.tensor_scalar(r0[:], pg[:, 0:B], b_sb[:, c:c + 1], None, ALU.add)
                                e.tensor_scalar(r1[:], pg[:, B:2 * B], b_sb[:, c + 4:c + 5], None, ALU.add)
                                e.tensor_mul(m0[:], r0[:], cts)
                                e.tensor_mul(m1[:], r1[:], sts)
                                e.tensor_sub(o_lo, m0[:], m1[:])
                                e.tensor_mul(m0[:], r0[:], sts)
                                e.tensor_mul(m1[:], r1[:], cts)
                                return e.tensor_add(o_hi, m0[:], m1[:])
                            P.add("vector", rope_fn)
                        if dbg and b == 0 and is_q and g == 1:
                            dct = s1.enter_context(nc.sbuf_tensor(pfx + "dct", [128, 1024], F32))
                            dct2 = s1.enter_context(nc.sbuf_tensor(pfx + "dct2", [128, 1024], BF16))
                            P.add("scalar", lambda e: e.copy(dct[:, 0:512], ct_sb[:, 0:512]))
                            P.add("vector", lambda e: e.tensor_copy(dct[:, 512:1024], pA[:, 0:512]))
                            P.add("vector", lambda e: e.tensor_copy(dct2[:, 0:512], wq_sb[:, 0:512]))
                            P.add("vector", lambda e: e.tensor_copy(dct2[:, 512:1024], lnxT[:, 0:512]))
                        if dbg and b == 0 and is_q and g == 2:
                            P.add("sync", lambda e: e.dma_start(out=d_ct, in_=dct[:, 0:512]), dma=True)
                            P.add("sync", lambda e: e.dma_start(out=d_qraw, in_=dct[:, 512:1024]), dma=True)
                            P.add("gpsimd", lambda e: e.dma_start(out=d_w[:, 0:512], in_=dct2[:, 0:512]), dma=True)
                            P.add("gpsimd", lambda e: e.dma_start(out=d_ln0, in_=dct2[:, 512:1024]), dma=True)
                        if is_q and g >= 2:
                            c = g - 2
                            par = (g - 2) % 2
                            P.add("sync", lambda e, c=c, par=par, t0=t0: e.dma_start(
                                out=qdr[c * 128:(c + 1) * 128, t0:t0 + B],
                                in_=q2[:, par * 1024:par * 1024 + 512]), dma=True)
                            P.add("sync", lambda e, c=c, par=par, t0=t0: e.dma_start(
                                out=qdr[(c + 4) * 128:(c + 5) * 128, t0:t0 + B],
                                in_=q2[:, par * 1024 + 512:par * 1024 + 1024]), dma=True)
                        P.ph()

        with contextlib.ExitStack() as s2:
            sb2 = lambda name, shape, dt=F32: s2.enter_context(nc.sbuf_tensor(pfx + name, shape, dt))
            wv_sb = sb2("wv_sb", [128, 8 * 1024], BF16)
            bv_sb = sb2("bv_sb", [1, 1024])
            lnxTb = sb2("lnxTb", [128, 8 * B], BF16)
            for dc in range(8):
                P.add("gpsimd", lambda e, dc=dc: e.dma_start(
                    out=wv_sb[:, dc * 1024:(dc + 1) * 1024], in_=wvt[dc * 128:(dc + 1) * 128, :]), dma=True)
            P.add("sync", lambda e: e.dma_start(out=bv_sb[:], in_=bv), dma=True)
            P.ph()
            for b in range(nbq if do_v else 0):
                t0 = b * B
                for dc in range(8):
                    P.add("sync", lambda e, dc=dc, t0=t0: e.dma_start(
                        out=lnxTb[:, dc * B:(dc + 1) * B],
                        in_=lndr[dc * 128:(dc + 1) * 128, t0:t0 + B]), dma=True)
                P.ph()
                for c in range(5):
                    if c < 4:
                        pg = [pA, pB][c % 2]
                        for hh in range(2):
                            pairs = [(lnxTb[:, dc * B + c * 128:dc * B + (c + 1) * 128],
                                      wv_sb[:, dc * 1024 + hh * 512:dc * 1024 + (hh + 1) * 512])
                                     for dc in range(8)]
                            def vmm(e, pairs=pairs, pg=pg, hh=hh):
                                last = None
                                for i, (l, rr) in enumerate(pairs):
                                    last = e.matmul(pg[:, hh * 512:(hh + 1) * 512], l, rr,
                                                    start=(i == 0), stop=False)
                                e.matmul(pg[:, hh * 512:(hh + 1) * 512], ones_col[:],
                                         bv_sb[:, hh * 512:(hh + 1) * 512], start=False, stop=False)
                                return fin(e, SNK)
                            P.add("tensor", vmm)
                    if c > 0:
                        pg = [pA, pB][(c - 1) % 2]
                        kc = (t0 // 128) + (c - 1)
                        P.add("scalar", lambda e, pg=pg, kc=kc: e.copy(
                            v_sb[:, kc * 1024:(kc + 1) * 1024], pg[:, 0:1024]))
                    P.ph()

        with contextlib.ExitStack() as s3:
            sb3 = lambda name, shape, dt=F32: s3.enter_context(nc.sbuf_tensor(pfx + name, shape, dt))
            qpan = sb3("qpan", [128, 8 * 512], BF16)
            probs = sb3("probs", [128, T], BF16)
            probsT = sb3("probsT", [128, T], BF16)
            rs_sb = sb3("rs_sb", [128, 8])
            xr_tm = sb3("xr_tm", [128, 1024])
            ot_tm = sb3("ot_tm", [128, 1024])

            for qp in range(8 if do_attn else 0):
                for dc in range(8):
                    P.add("sync", lambda e, dc=dc, qp=qp: e.dma_start(
                        out=qpan[:, dc * 512:(dc + 1) * 512],
                        in_=qdr[dc * 128:(dc + 1) * 128, qp * 512:(qp + 1) * 512]), dma=True)
                P.ph()
                for qc4 in range(4):
                    qc = qp * 4 + qc4
                    for kq in range(5):
                        if kq < 4:
                            pg = [pA, pB][kq % 2]
                            for hh in range(2):
                                pairs = [(qpan[:, dc * 512 + qc4 * 128:dc * 512 + (qc4 + 1) * 128],
                                          k_sb[:, dc * T + kq * 1024 + hh * 512:dc * T + kq * 1024 + (hh + 1) * 512])
                                         for dc in range(8)]
                                P.add("tensor", mm_group(pg[:, hh * 512:(hh + 1) * 512], pairs, SNK))
                        if kq > 0:
                            pg = [pA, pB][(kq - 1) % 2]
                            P.add("scalar", lambda e, kq=kq, pg=pg: e.activation(
                                probs[:, (kq - 1) * 1024:kq * 1024], pg[:, 0:1024], AF.Exp,
                                scale=cst[:, 1:2], accum_out=rs_sb[:, kq - 1:kq]))
                        P.ph()
                    P.add("vector", lambda e: e.reduce_sum(rs_sb[:, 4:5], rs_sb[:, 0:4], axis=AX.X))
                    for r in range(4):
                        items = [(pTR[:, j * 128:(j + 1) * 128],
                                  probs[:, (r * 8 + j) * 128:(r * 8 + j + 1) * 128]) for j in range(8)]
                        P.add("tensor", tr_group(items, idt[:], SNK))
                        P.ph()
                        def ptrc_fn(e, r=r):
                            last = None
                            for j in range(8):
                                kc = r * 8 + j
                                last = e.tensor_copy(probsT[:, kc * 128:(kc + 1) * 128],
                                                     pTR[:, j * 128:(j + 1) * 128])
                            return last
                        P.add("vector", ptrc_fn)
                        if r == 0:
                            P.add("sync", lambda e, qc=qc: e.dma_start(
                                out=xr_tm[:], in_=x[qc * 128:(qc + 1) * 128, :]), dma=True)
                        P.ph()
                    if dbg and qc == 0:
                        dcp = sb3("dcp", [128, T])
                        P.add("vector", lambda e: e.tensor_copy(dcp[:], probs[:]))
                        P.ph()
                        P.add("sync", lambda e: e.dma_start(out=d_probs, in_=dcp[:]), dma=True)
                        P.add("sync", lambda e: e.dma_start(out=d_rs, in_=rs_sb[:]), dma=True)
                        P.ph()
                        P.add("vector", lambda e: e.tensor_copy(dcp[:, 0:4096], k_sb[:, 0:4096]))
                        P.ph()
                        P.add("sync", lambda e: e.dma_start(out=d_k, in_=dcp[:, 0:4096]), dma=True)
                        P.ph()
                        P.add("vector", lambda e: e.tensor_copy(dcp[:, 0:1024], v_sb[:, 0:1024]))
                        P.ph()
                        P.add("sync", lambda e: e.dma_start(out=d_v, in_=dcp[:, 0:1024]), dma=True)
                        P.ph()
                        P.add("vector", lambda e: e.tensor_copy(dcp[:, 0:512], qpan[:, 0:512]))
                        P.ph()
                        P.add("sync", lambda e: e.dma_start(out=d_q, in_=dcp[:, 0:512]), dma=True)
                        P.ph()
                    P.add("vector", lambda e: e.reciprocal(rs_sb[:, 5:6], rs_sb[:, 4:5]))
                    P.ph()
                    def pv_fn(e):
                        for kc in range(32):
                            for hh in range(2):
                                e.matmul(pPV[:, hh * 512:(hh + 1) * 512],
                                         probsT[:, kc * 128:(kc + 1) * 128],
                                         v_sb[:, kc * 1024 + hh * 512:kc * 1024 + (hh + 1) * 512],
                                         start=(kc == 0), stop=False)
                        return fin(e, SNK)
                    P.add("tensor", pv_fn)
                    P.ph()
                    P.add("vector", lambda e: e.scalar_tensor_tensor(
                        ot_tm[:], pPV[:, 0:1024], rs_sb[:, 5:6], xr_tm[:], ALU.mult, ALU.add))
                    P.ph()
                    P.add("sync", lambda e, qc=qc: e.dma_start(
                        out=xo[qc * 128:(qc + 1) * 128, :], in_=ot_tm[:]), dma=True)
                    P.ph()


def build_attn0(dbg=False, nbq=8, do_k=True, do_v=True, do_attn=True):
    import contextlib
    nc = bass.Bass("TRN2", target_bir_lowering=False, debug=False)
    x = nc.dram_tensor("x", [T, D], F32, kind="ExternalInput").ap()
    ins = declare_attn0_ins(nc)
    xo = nc.dram_tensor("xo", [T, D], F32, kind="ExternalOutput").ap()
    with contextlib.ExitStack() as stack:
        sc = stack.enter_context(nc.semaphore("sc"))
        sd = stack.enter_context(nc.semaphore("sd"))
        P = PH()
        with contextlib.ExitStack() as inner:
            emit_attn0(nc, P, inner, "", ins, x, xo, dbg=dbg, nbq=nbq, do_k=do_k, do_v=do_v, do_attn=do_attn)
        with nc.Block() as block:
            P.finalize(nc, block, sc, sd)
    return nc


def attn0_host_inputs(inputs, x_full):
    perm = np.concatenate([np.arange(0, 1024, 2), np.arange(1, 1024, 2)])
    wq = np.asarray(inputs["L0_attn_b_wq_w"], np.float32)[perm]
    wk = np.asarray(inputs["L0_attn_b_wk_w"], np.float32)[perm]
    wv = np.asarray(inputs["L0_attn_b_wv_w"], np.float32)
    bqv = np.asarray(inputs["L0_attn_b_wq_b"], np.float32)[perm]
    bkv = np.asarray(inputs["L0_attn_b_wk_b"], np.float32)[perm]
    bvv = np.asarray(inputs["L0_attn_b_wv_b"], np.float32)
    inv_freq = 1.0 / (10000.0 ** (np.arange(0, 1024, 2, dtype=np.float64) / 1024.0))
    ang = np.arange(T, dtype=np.float64)[:, None] * inv_freq[None, :]
    cosT = np.cos(ang).T.astype(np.float32)
    sinT = np.sin(ang).T.astype(np.float32)
    shared = dict(
        wqt0=np.ascontiguousarray(wq.T), wkt0=np.ascontiguousarray(wk.T),
        wvt0=np.ascontiguousarray(wv.T),
        bq0=np.ascontiguousarray(bqv.reshape(8, 128).T), bk0=np.ascontiguousarray(bkv.reshape(8, 128).T),
        bv0=np.ascontiguousarray(bvv.reshape(1, 1024)),
        cosT=np.ascontiguousarray(cosT), sinT=np.ascontiguousarray(sinT))
    return [dict(shared, x=np.ascontiguousarray(x_full[c])) for c in range(NCORES)]



MHA1_INS = ["wqt1", "wkt1", "wvt1", "wot1"]


def declare_mha1_ins(nc):
    return {k: nc.dram_tensor(k, [1024, 1024], F32, kind="ExternalInput").ap() for k in MHA1_INS}


def emit_mha1(nc, P, stack, pfx, ins, x_in, x_out):
    x = x_in
    xo = x_out
    wqt, wkt, wvt, wot = (ins[k] for k in MHA1_INS)
    qdr = nc.dram_tensor(pfx + "qdr", [1024, T], BF16, kind="Internal").ap()

    NBQ = 8
    B = 512

    import contextlib
    if True:
        sb = lambda name, shape, dt=F32: stack.enter_context(nc.sbuf_tensor(pfx + name, shape, dt))
        psum = lambda name, shape, dt=F32: stack.enter_context(nc.psum_tensor(pfx + name, shape, dt))

        k_sb = sb("k_sb", [128, 8 * T], BF16)
        v_sb = sb("v_sb", [128, 32 * 1024], BF16)
        cst = sb("cst", [128, 2])
        zb = sb("zbh", [128, 128], BF16)
        mtri = sb("mtri", [128, 128])
        pS0 = psum("pS0", [128, 512])
        pS1 = psum("pS1", [128, 512])
        pOut = psum("pOut", [128, 1024])
        pTR = psum("pTR", [128, 2048], BF16)

        idt = build_identity(P, nc, stack, pfx)
        SNK = make_sink(nc, P, stack, pfx)
        P.add("vector", lambda e: e.memset(cst[:, 0:1], 1e-5))
        P.add("vector", lambda e: e.memset(cst[:, 1:2], 1.0 / 8.0))
        P.add("gpsimd", lambda e: e.memset(mtri[:], 0.0))
        P.ph()
        P.add("gpsimd", lambda e: e.affine_select(
            mtri[:], mtri[:], [[-1, 128]], ALU.is_ge, -1e9, base=0, channel_multiplier=1))

        with contextlib.ExitStack() as s1:
            sb1 = lambda name, shape, dt=F32: s1.enter_context(nc.sbuf_tensor(pfx + name, shape, dt))
            wq_sb = sb1("wq_sb", [128, 8 * 1024], BF16)
            wk_sb = sb1("wk_sb", [128, 8 * 1024], BF16)
            wv_sb = sb1("wv_sb", [128, 8 * 1024], BF16)
            x_tm = sb1("x_tm", [128, 2 * 1024])
            sq_sc = sb1("sq_sc", [128, 1024], BF16)
            st_sc = sb1("st_sc", [128, 16])
            lnx = sb1("lnx", [128, 4 * 1024], BF16)
            lnxT = sb1("lnxT", [128, 8 * B], BF16)
            q2 = sb1("q2", [128, 2 * B], BF16)

            for dc in range(8):
                P.add("gpsimd", lambda e, dc=dc: e.dma_start(
                    out=wq_sb[:, dc * 1024:(dc + 1) * 1024], in_=wqt[dc * 128:(dc + 1) * 128, :]), dma=True)
                P.add("gpsimd", lambda e, dc=dc: e.dma_start(
                    out=wk_sb[:, dc * 1024:(dc + 1) * 1024], in_=wkt[dc * 128:(dc + 1) * 128, :]), dma=True)
                P.add("gpsimd", lambda e, dc=dc: e.dma_start(
                    out=wv_sb[:, dc * 1024:(dc + 1) * 1024], in_=wvt[dc * 128:(dc + 1) * 128, :]), dma=True)
            P.ph()

            for b in range(NBQ):
                t0 = b * B
                for hf in range(2):
                    P.add("sync", lambda e, t0=t0, hf=hf: e.dma_start(
                        out=x_tm[:].rearrange("p (c d) -> p c d", c=2),
                        in_=x[t0 + hf * 256:t0 + (hf + 1) * 256, :].rearrange("(c p) d -> p c d", p=128)), dma=True)
                    P.ph()
                    emit_ln(P, nc, x_tm, lnx[:, hf * 2048:(hf + 1) * 2048], sq_sc, st_sc, cst[:, 0:1], 2)
                for c in range(4):
                    items = [(pTR[:, dc * 128:(dc + 1) * 128],
                              lnx[:, c * 1024 + dc * 128:c * 1024 + (dc + 1) * 128])
                             for dc in range(8)]
                    P.add("tensor", tr_group(items, idt[:], SNK))
                    P.ph()
                    def trc_fn(e, c=c):
                        last = None
                        for dc in range(8):
                            last = e.tensor_copy(lnxT[:, dc * B + c * 128:dc * B + (c + 1) * 128],
                                                 pTR[:, dc * 128:(dc + 1) * 128])
                        return last
                    P.add("vector", trc_fn)
                    P.ph()
                for (w_sb, is_q) in ((wq_sb, True), (wk_sb, False)):
                    for g in range(10):
                        if g < 8:
                            pg = [pS0, pS1][g % 2]
                            pairs = [(w_sb[:, dc * 1024 + g * 128:dc * 1024 + (g + 1) * 128],
                                      lnxT[:, dc * B:(dc + 1) * B]) for dc in range(8)]
                            P.add("tensor", mm_group(pg[:, 0:B], pairs, SNK))
                        if 1 <= g <= 8:
                            pg = [pS0, pS1][(g - 1) % 2]
                            if is_q:
                                par = (g - 1) % 2
                                P.add("scalar", lambda e, pg=pg, par=par: e.copy(
                                    q2[:, par * B:(par + 1) * B], pg[:, 0:B]))
                            else:
                                P.add("scalar", lambda e, g=g, pg=pg, t0=t0: e.copy(
                                    k_sb[:, (g - 1) * T + t0:(g - 1) * T + t0 + B], pg[:, 0:B]))
                        if is_q and 2 <= g <= 9:
                            dc = g - 2
                            par = (g - 2) % 2
                            P.add("sync", lambda e, dc=dc, par=par, t0=t0: e.dma_start(
                                out=qdr[dc * 128:(dc + 1) * 128, t0:t0 + B],
                                in_=q2[:, par * B:(par + 1) * B]), dma=True)
                        P.ph()
                for c in range(5):
                    if c < 4:
                        pg = [pS0, pS1][c % 2]
                        pairs = [(lnxT[:, dc * B + c * 128:dc * B + (c + 1) * 128],
                                  wv_sb[:, dc * 1024:(dc + 1) * 1024]) for dc in range(8)]
                        def vmm(e, pairs=pairs, pg=pg):
                            for i, (l, rr) in enumerate(pairs):
                                e.matmul(pg[:, 0:512], l, rr[:, 0:512], start=(i == 0), stop=False)
                            for i, (l, rr) in enumerate(pairs):
                                e.matmul(pOut[:, 0:512], l, rr[:, 512:1024], start=(i == 0), stop=False)
                            return fin(e, SNK)
                        P.add("tensor", vmm)
                    if c > 0:
                        pg = [pS0, pS1][(c - 1) % 2]
                        kc = (t0 // 128) + (c - 1)
                        def vev(e, pg=pg, kc=kc):
                            e.copy(v_sb[:, kc * 1024:kc * 1024 + 512], pg[:, 0:512])
                            return e.copy(v_sb[:, kc * 1024 + 512:(kc + 1) * 1024], pOut[:, 0:512])
                        P.add("scalar", vev)
                    P.ph()

        with contextlib.ExitStack() as s2:
            sb2 = lambda name, shape, dt=F32: s2.enter_context(nc.sbuf_tensor(pfx + name, shape, dt))
            wo_sb = sb2("wo_sb", [128, 8 * 1024], BF16)
            qpan = sb2("qpan", [128, 8 * 512], BF16)
            probs = sb2("probs", [128, 2 * T], BF16)
            probsT = sb2("probsT", [128, T], BF16)
            rsac = sb2("rsac", [128, 16 * 8])
            rtot = sb2("rtot", [128, 16])
            rrec = sb2("rrec", [128, 16])
            tri01 = sb2("tri01", [128, 128], BF16)
            out_bf = sb2("out_bf", [128, 1024], BF16)
            outT = sb2("outT", [128, 1024], BF16)
            xr_tm = sb2("xr_tm", [128, 1024])
            ot_tm = sb2("ot_tm", [128, 1024])

            for dc in range(8):
                P.add("gpsimd", lambda e, dc=dc: e.dma_start(
                    out=wo_sb[:, dc * 1024:(dc + 1) * 1024], in_=wot[dc * 128:(dc + 1) * 128, :]), dma=True)
            P.add("gpsimd", lambda e: e.memset(tri01[:], 1.0))
            P.ph()
            P.add("gpsimd", lambda e: e.affine_select(
                tri01[:], tri01[:], [[-1, 128]], ALU.is_ge, 0.0, base=0, channel_multiplier=1))
            P.ph()

            for qc in range(32):
                qp, qc4 = qc // 4, qc % 4
                if qc4 == 0:
                    for dc in range(8):
                        P.add("sync", lambda e, dc=dc, qp=qp: e.dma_start(
                            out=qpan[:, dc * 512:(dc + 1) * 512],
                            in_=qdr[dc * 128:(dc + 1) * 128, qp * 512:(qp + 1) * 512]), dma=True)
                    P.ph()
                kl = (qc + 1) * 128
                nslab = (kl + 511) // 512
                def mk_tail(h, ntile, kl, nslab):
                    half = (h % 2) * 4096
                    dslab = nslab - 1
                    dk0 = dslab * 512
                    dwd = kl - dk0
                    S = {0: [], 1: [], 2: [], 3: [], 4: [], 5: []}

                    def dmask(e, h=h, half=half, dk0=dk0, dwd=dwd, kl=kl, dslab=dslab):
                        e.tensor_mul(probs[:, half + kl - 128:half + kl],
                                     probs[:, half + kl - 128:half + kl], tri01[:])
                        return e.reduce_sum(rsac[:, h * 8 + dslab:h * 8 + dslab + 1],
                                            probs[:, half + dk0:half + dk0 + dwd], axis=AX.X)
                    S[0].append(("vector", dmask))
                    rounds = [(0, min(16, ntile))]
                    if ntile > 16:
                        rounds.append((16, ntile - 16))
                    for ri, (r0, rn) in enumerate(rounds):
                        items = [(pTR[:, j * 128:(j + 1) * 128],
                                  probs[:, half + (r0 + j) * 128:half + (r0 + j + 1) * 128])
                                 for j in range(rn)]
                        S[1 + 2 * ri].append(("tensor", tr_group(items, idt[:], SNK)))

                        def ptrc(e, r0=r0, rn=rn):
                            last = None
                            for j in range(rn):
                                last = e.tensor_copy(probsT[:, (r0 + j) * 128:(r0 + j + 1) * 128],
                                                     pTR[:, j * 128:(j + 1) * 128])
                            return last
                        S[2 + 2 * ri].append(("vector", ptrc))

                    def pv(e, h=h, ntile=ntile):
                        col = 64 * h
                        bank_start = (h == 0 or h == 8)
                        for kc in range(ntile):
                            e.matmul(pOut[:, col:col + 64],
                                     probsT[:, kc * 128:(kc + 1) * 128],
                                     v_sb[:, kc * 1024 + col:kc * 1024 + col + 64],
                                     start=(bank_start and kc == 0), stop=(kc == ntile - 1))
                        return fin(e, SNK)
                    S[5].append(("tensor", pv))
                    return S

                ntile = kl // 128
                prev_tail = None
                for h in range(17):
                    NP = max(nslab + 1, 6)
                    for i in range(NP):
                        if h < 16:
                            po = 64 * (h % 2)
                            hc = h // 2
                            half = (h % 2) * 4096
                            if i < nslab:
                                k0 = i * 512
                                w = min(512, kl - k0)
                                pg = [pS0, pS1][i % 2]
                                def smm(e, po=po, hc=hc, k0=k0, w=w, pg=pg, qc4=qc4):
                                    e.matmul(pg[:, 0:w],
                                             qpan[po:po + 64, hc * 512 + qc4 * 128:hc * 512 + (qc4 + 1) * 128],
                                             k_sb[po:po + 64, hc * T + k0:hc * T + k0 + w],
                                             start=True, stop=True)
                                    return fin(e, SNK)
                                P.add("tensor", smm)
                            if 1 <= i <= nslab:
                                k0p = (i - 1) * 512
                                wp = min(512, kl - k0p)
                                pg = [pS0, pS1][(i - 1) % 2]
                                is_diag = (k0p + wp == kl)
                                if is_diag:
                                    P.add("scalar", lambda e, half=half, k0p=k0p, wp=wp, pg=pg: e.activation(
                                        probs[:, half + k0p:half + k0p + wp], pg[:, 0:wp], AF.Exp,
                                        scale=cst[:, 1:2]))
                                else:
                                    P.add("scalar", lambda e, h=h, i=i, half=half, k0p=k0p, wp=wp, pg=pg:
                                          e.activation(probs[:, half + k0p:half + k0p + wp], pg[:, 0:wp],
                                                       AF.Exp, scale=cst[:, 1:2],
                                                       accum_out=rsac[:, h * 8 + (i - 1):h * 8 + i]))
                        if prev_tail is not None and i in prev_tail:
                            for eng, fn in prev_tail[i]:
                                P.add(eng, fn)
                        P.ph()
                    prev_tail = mk_tail(h, ntile, kl, nslab) if h < 16 else None
                def rfin(e, nslab=nslab):
                    last = None
                    for h in range(16):
                        last = e.reduce_sum(rtot[:, h:h + 1], rsac[:, h * 8:h * 8 + nslab], axis=AX.X)
                    return last
                P.add("vector", rfin)
                P.add("sync", lambda e, qc=qc: e.dma_start(
                    out=xr_tm[:], in_=x[qc * 128:(qc + 1) * 128, :]), dma=True)
                P.ph()
                P.add("vector", lambda e: e.reciprocal(rrec[:], rtot[:]))
                P.ph()
                def onorm(e):
                    last = None
                    for h in range(16):
                        last = e.tensor_scalar(out_bf[:, 64 * h:64 * h + 64], pOut[:, 64 * h:64 * h + 64],
                                               rrec[:, h:h + 1], None, ALU.mult)
                    return last
                P.add("vector", onorm)
                P.ph()
                items = [(pTR[:, ec * 128:(ec + 1) * 128], out_bf[:, ec * 128:(ec + 1) * 128])
                         for ec in range(8)]
                P.add("tensor", tr_group(items, idt[:], SNK))
                P.ph()
                def otc(e):
                    last = None
                    for ec in range(8):
                        last = e.tensor_copy(outT[:, ec * 128:(ec + 1) * 128], pTR[:, ec * 128:(ec + 1) * 128])
                    return last
                P.add("vector", otc)
                P.ph()
                for hh in range(2):
                    pairs = [(outT[:, ec * 128:(ec + 1) * 128],
                              wo_sb[:, ec * 1024 + hh * 512:ec * 1024 + (hh + 1) * 512]) for ec in range(8)]
                    P.add("tensor", mm_group([pS0, pS1][hh][:, 0:512], pairs, SNK))
                P.ph()
                def resadd(e):
                    e.tensor_add(ot_tm[:, 0:512], pS0[:, 0:512], xr_tm[:, 0:512])
                    return e.tensor_add(ot_tm[:, 512:1024], pS1[:, 0:512], xr_tm[:, 512:1024])
                P.add("vector", resadd)
                P.ph()
                P.add("sync", lambda e, qc=qc: e.dma_start(
                    out=xo[qc * 128:(qc + 1) * 128, :], in_=ot_tm[:]), dma=True)
                P.ph()


def build_mha1():
    import contextlib
    nc = bass.Bass("TRN2", target_bir_lowering=False, debug=False)
    x = nc.dram_tensor("x", [T, D], F32, kind="ExternalInput").ap()
    ins = declare_mha1_ins(nc)
    xo = nc.dram_tensor("xo", [T, D], F32, kind="ExternalOutput").ap()
    with contextlib.ExitStack() as stack:
        sc = stack.enter_context(nc.semaphore("sc"))
        sd = stack.enter_context(nc.semaphore("sd"))
        P = PH()
        with contextlib.ExitStack() as inner:
            emit_mha1(nc, P, inner, "", ins, x, xo)
        with nc.Block() as block:
            P.finalize(nc, block, sc, sd)
    return nc


def mha1_host_inputs(inputs, x_full):
    shared = dict(
        wqt1=np.ascontiguousarray(np.asarray(inputs["L1_attn_a_Wq"], np.float32).T),
        wkt1=np.ascontiguousarray(np.asarray(inputs["L1_attn_a_Wk"], np.float32).T),
        wvt1=np.ascontiguousarray(np.asarray(inputs["L1_attn_a_Wv"], np.float32).T),
        wot1=np.ascontiguousarray(np.asarray(inputs["L1_attn_a_Wo"], np.float32).T))
    return [dict(shared, x=np.ascontiguousarray(x_full[c])) for c in range(NCORES)]



SWA1_INS = ["wqt2", "wkt2", "wvt2", "wot2", "bq2", "bk2", "bv2", "bo2"]


def declare_swa1_ins(nc):
    d = {k: nc.dram_tensor(k, [1024, 1024], F32, kind="ExternalInput").ap()
         for k in ["wqt2", "wkt2", "wvt2", "wot2"]}
    d["bq2"] = nc.dram_tensor("bq2", [128, 8], F32, kind="ExternalInput").ap()
    d["bk2"] = nc.dram_tensor("bk2", [128, 8], F32, kind="ExternalInput").ap()
    d["bv2"] = nc.dram_tensor("bv2", [1, 1024], F32, kind="ExternalInput").ap()
    d["bo2"] = nc.dram_tensor("bo2", [1, 1024], F32, kind="ExternalInput").ap()
    return d


def emit_swa1(nc, P, stack, pfx, ins, x_in, x_out):
    x = x_in
    xo = x_out
    wqt, wkt, wvt, wot, bq, bk, bv, bo = (ins[k] for k in SWA1_INS)
    qdr = nc.dram_tensor(pfx + "qdr", [1024, T], BF16, kind="Internal").ap()

    NBQ = 8
    B = 512

    import contextlib
    if True:
        sb = lambda name, shape, dt=F32: stack.enter_context(nc.sbuf_tensor(pfx + name, shape, dt))
        psum = lambda name, shape, dt=F32: stack.enter_context(nc.psum_tensor(pfx + name, shape, dt))

        k_sb = sb("k_sb", [128, 8 * T], BF16)
        v_sb = sb("v_sb", [128, 32 * 1024], BF16)
        cst = sb("cst", [128, 2])
        zb = sb("zbs", [128, 128], BF16)
        ones_col = sb("ones_col_s", [1, 128])
        tri01 = sb("tri01", [128, 128], BF16)
        stri01 = sb("stri01", [128, 128], BF16)
        pS0 = psum("pS0", [128, 1024])
        pS1 = psum("pS1", [128, 1024])
        pOut = psum("pOut", [128, 1024])
        pTR = psum("pTR", [128, 1024], BF16)

        idt = build_identity(P, nc, stack, pfx)
        SNK = make_sink(nc, P, stack, pfx)
        P.add("vector", lambda e: e.memset(cst[:, 0:1], 1e-5))
        P.add("vector", lambda e: e.memset(cst[:, 1:2], 1.0 / 8.0))
        P.add("vector", lambda e: e.memset(ones_col[:], 1.0))
        P.add("gpsimd", lambda e: e.memset(tri01[:], 1.0))
        P.add("gpsimd", lambda e: e.memset(stri01[:], 1.0))
        P.ph()
        P.add("gpsimd", lambda e: e.affine_select(
            tri01[:], tri01[:], [[-1, 128]], ALU.is_ge, 0.0, base=0, channel_multiplier=1))
        P.add("gpsimd", lambda e: e.affine_select(
            stri01[:], stri01[:], [[1, 128]], ALU.is_ge, 0.0, base=-1, channel_multiplier=-1))
        P.ph()

        with contextlib.ExitStack() as s1:
            sb1 = lambda name, shape, dt=F32: s1.enter_context(nc.sbuf_tensor(pfx + name, shape, dt))
            wq_sb = sb1("wq_sb", [128, 8 * 1024], BF16)
            wk_sb = sb1("wk_sb", [128, 8 * 1024], BF16)
            wv_sb = sb1("wv_sb", [128, 8 * 1024], BF16)
            bq_sb = sb1("bq_sb", [128, 8])
            bk_sb = sb1("bk_sb", [128, 8])
            bv_sb = sb1("bv_sb", [1, 1024])
            x_tm = sb1("x_tm", [128, 1024])
            sq_sc = sb1("sq_sc", [128, 1024], BF16)
            st_sc = sb1("st_sc", [128, 16])
            lnx = sb1("lnx", [128, 4 * 1024], BF16)
            lnxT = sb1("lnxT", [128, 8 * B], BF16)
            q2 = sb1("q2", [128, 2 * B], BF16)

            for dc in range(8):
                P.add("gpsimd", lambda e, dc=dc: e.dma_start(
                    out=wq_sb[:, dc * 1024:(dc + 1) * 1024], in_=wqt[dc * 128:(dc + 1) * 128, :]), dma=True)
                P.add("gpsimd", lambda e, dc=dc: e.dma_start(
                    out=wk_sb[:, dc * 1024:(dc + 1) * 1024], in_=wkt[dc * 128:(dc + 1) * 128, :]), dma=True)
                P.add("gpsimd", lambda e, dc=dc: e.dma_start(
                    out=wv_sb[:, dc * 1024:(dc + 1) * 1024], in_=wvt[dc * 128:(dc + 1) * 128, :]), dma=True)
            P.add("sync", lambda e: e.dma_start(out=bq_sb[:], in_=bq), dma=True)
            P.add("sync", lambda e: e.dma_start(out=bk_sb[:], in_=bk), dma=True)
            P.add("sync", lambda e: e.dma_start(out=bv_sb[:], in_=bv), dma=True)
            P.ph()

            for b in range(NBQ):
                t0 = b * B
                for hf in range(4):
                    P.add("sync", lambda e, t0=t0, hf=hf: e.dma_start(
                        out=x_tm[:], in_=x[t0 + hf * 128:t0 + (hf + 1) * 128, :]), dma=True)
                    P.ph()
                    emit_ln(P, nc, x_tm, lnx[:, hf * 1024:(hf + 1) * 1024], sq_sc, st_sc, cst[:, 0:1], 1)
                for c in range(4):
                    items = [(pTR[:, dc * 128:(dc + 1) * 128],
                              lnx[:, c * 1024 + dc * 128:c * 1024 + (dc + 1) * 128])
                             for dc in range(8)]
                    P.add("tensor", tr_group(items, idt[:], SNK))
                    P.ph()
                    def trc_fn(e, c=c):
                        last = None
                        for dc in range(8):
                            last = e.tensor_copy(lnxT[:, dc * B + c * 128:dc * B + (c + 1) * 128],
                                                 pTR[:, dc * 128:(dc + 1) * 128])
                        return last
                    P.add("vector", trc_fn)
                    P.ph()
                for (w_sb, b_sbx, is_q) in ((wq_sb, bq_sb, True), (wk_sb, bk_sb, False)):
                    for g in range(10):
                        if g < 8:
                            pg = [pS0, pS1][g % 2]
                            pairs = [(w_sb[:, dc * 1024 + g * 128:dc * 1024 + (g + 1) * 128],
                                      lnxT[:, dc * B:(dc + 1) * B]) for dc in range(8)]
                            P.add("tensor", mm_group(pg[:, 0:B], pairs, SNK))
                        if 1 <= g <= 8:
                            pg = [pS0, pS1][(g - 1) % 2]
                            if is_q:
                                par = (g - 1) % 2
                                P.add("scalar", lambda e, g=g, pg=pg, par=par, b_sbx=b_sbx: e.activation(
                                    q2[:, par * B:(par + 1) * B], pg[:, 0:B], AF.Identity,
                                    bias=b_sbx[:, g - 1:g]))
                            else:
                                P.add("scalar", lambda e, g=g, pg=pg, t0=t0, b_sbx=b_sbx: e.activation(
                                    k_sb[:, (g - 1) * T + t0:(g - 1) * T + t0 + B], pg[:, 0:B], AF.Identity,
                                    bias=b_sbx[:, g - 1:g]))
                        if is_q and 2 <= g <= 9:
                            dc = g - 2
                            par = (g - 2) % 2
                            P.add("sync", lambda e, dc=dc, par=par, t0=t0: e.dma_start(
                                out=qdr[dc * 128:(dc + 1) * 128, t0:t0 + B],
                                in_=q2[:, par * B:(par + 1) * B]), dma=True)
                        P.ph()
                for c in range(5):
                    if c < 4:
                        pg = [pS0, pS1][c % 2]
                        pairs = [(lnxT[:, dc * B + c * 128:dc * B + (c + 1) * 128],
                                  wv_sb[:, dc * 1024:(dc + 1) * 1024]) for dc in range(8)]
                        def vmm(e, pairs=pairs, pg=pg):
                            for i, (l, rr) in enumerate(pairs):
                                e.matmul(pg[:, 0:512], l, rr[:, 0:512], start=(i == 0), stop=False)
                            for i, (l, rr) in enumerate(pairs):
                                e.matmul(pOut[:, 0:512], l, rr[:, 512:1024], start=(i == 0), stop=False)
                            e.matmul(pg[:, 0:512], ones_col[:], bv_sb[:, 0:512], start=False, stop=False)
                            e.matmul(pOut[:, 0:512], ones_col[:], bv_sb[:, 512:1024], start=False, stop=False)
                            return fin(e, SNK)
                        P.add("tensor", vmm)
                    if c > 0:
                        pg = [pS0, pS1][(c - 1) % 2]
                        kc = (t0 // 128) + (c - 1)
                        def vev(e, pg=pg, kc=kc):
                            e.copy(v_sb[:, kc * 1024:kc * 1024 + 512], pg[:, 0:512])
                            return e.copy(v_sb[:, kc * 1024 + 512:(kc + 1) * 1024], pOut[:, 0:512])
                        P.add("scalar", vev)
                    P.ph()

        with contextlib.ExitStack() as s2:
            sb2 = lambda name, shape, dt=F32: s2.enter_context(nc.sbuf_tensor(pfx + name, shape, dt))
            wo_sb = sb2("wo_sb", [128, 8 * 1024], BF16)
            bo_sb = sb2("bo_sb", [1, 1024])
            qpan = sb2("qpan", [128, 8 * 512], BF16)
            probs = sb2("probs", [128, 2 * 640], BF16)
            probsT = sb2("probsT", [128, 640], BF16)
            rtot = sb2("rtot", [128, 16])
            rrec = sb2("rrec", [128, 16])
            out_bf = sb2("out_bf", [128, 1024], BF16)
            outT = sb2("outT", [128, 1024], BF16)
            xr_tm = sb2("xr_tm", [128, 1024])
            ot_tm = sb2("ot_tm", [128, 1024])

            for dc in range(8):
                P.add("gpsimd", lambda e, dc=dc: e.dma_start(
                    out=wo_sb[:, dc * 1024:(dc + 1) * 1024], in_=wot[dc * 128:(dc + 1) * 128, :]), dma=True)
            P.add("sync", lambda e: e.dma_start(out=bo_sb[:], in_=bo), dma=True)
            P.ph()

            for qc in range(32):
                qp, qc4 = qc // 4, qc % 4
                if qc4 == 0:
                    for dc in range(8):
                        P.add("sync", lambda e, dc=dc, qp=qp: e.dma_start(
                            out=qpan[:, dc * 512:(dc + 1) * 512],
                            in_=qdr[dc * 128:(dc + 1) * 128, qp * 512:(qp + 1) * 512]), dma=True)
                    P.ph()
                kc0 = max(0, qc - 4)
                ntile = qc - kc0 + 1
                w = ntile * 128
                def mk_tail(h, kc0, ntile, w):
                    half = (h % 2) * 640
                    S = {0: [], 1: [], 2: [], 3: []}

                    def dmask(e, h=h, half=half, w=w, qc=qc, kc0=kc0):
                        e.tensor_mul(probs[:, half + w - 128:half + w],
                                     probs[:, half + w - 128:half + w], tri01[:])
                        if qc - kc0 == 4:
                            e.tensor_mul(probs[:, half:half + 128], probs[:, half:half + 128], stri01[:])
                        return e.reduce_sum(rtot[:, h:h + 1], probs[:, half:half + w], axis=AX.X)
                    S[0].append(("vector", dmask))
                    items = [(pTR[:, j * 128:(j + 1) * 128], probs[:, half + j * 128:half + (j + 1) * 128])
                             for j in range(ntile)]
                    S[1].append(("tensor", tr_group(items, idt[:], SNK)))

                    def ptrc(e, ntile=ntile):
                        last = None
                        for j in range(ntile):
                            last = e.tensor_copy(probsT[:, j * 128:(j + 1) * 128],
                                                 pTR[:, j * 128:(j + 1) * 128])
                        return last
                    S[2].append(("vector", ptrc))

                    def pv(e, h=h, kc0=kc0, ntile=ntile):
                        col = 64 * h
                        bank_start = (h == 0 or h == 8)
                        for j in range(ntile):
                            kc = kc0 + j
                            e.matmul(pOut[:, col:col + 64],
                                     probsT[:, j * 128:(j + 1) * 128],
                                     v_sb[:, kc * 1024 + col:kc * 1024 + col + 64],
                                     start=(bank_start and j == 0), stop=(j == ntile - 1))
                        return fin(e, SNK)
                    S[3].append(("tensor", pv))
                    return S

                prev_tail = None
                for h in range(17):
                    for i in range(4):
                        if h < 16:
                            po = 64 * (h % 2)
                            hc = h // 2
                            half = (h % 2) * 640
                            pg = [pS0, pS1][h % 2]
                            if i == 0:
                                def smm(e, po=po, hc=hc, kc0=kc0, w=w, pg=pg, qc4=qc4):
                                    k0 = kc0 * 128
                                    e.matmul(pg[:, 0:min(512, w)],
                                             qpan[po:po + 64, hc * 512 + qc4 * 128:hc * 512 + (qc4 + 1) * 128],
                                             k_sb[po:po + 64, hc * T + k0:hc * T + k0 + min(512, w)],
                                             start=True, stop=(w <= 512))
                                    if w > 512:
                                        e.matmul(pg[:, 512:w],
                                                 qpan[po:po + 64, hc * 512 + qc4 * 128:hc * 512 + (qc4 + 1) * 128],
                                                 k_sb[po:po + 64, hc * T + k0 + 512:hc * T + k0 + w],
                                                 start=True, stop=True)
                                    return fin(e, SNK)
                                P.add("tensor", smm)
                            if i == 1:
                                P.add("scalar", lambda e, pg=pg, half=half, w=w: e.activation(
                                    probs[:, half:half + w], pg[:, 0:w], AF.Exp, scale=cst[:, 1:2]))
                        if prev_tail is not None and i in prev_tail:
                            for eng, fn in prev_tail[i]:
                                P.add(eng, fn)
                        P.ph()
                    prev_tail = mk_tail(h, kc0, ntile, w) if h < 16 else None
                P.add("sync", lambda e, qc=qc: e.dma_start(
                    out=xr_tm[:], in_=x[qc * 128:(qc + 1) * 128, :]), dma=True)
                P.add("vector", lambda e: e.reciprocal(rrec[:], rtot[:]))
                P.ph()
                def onorm(e):
                    last = None
                    for h in range(16):
                        last = e.tensor_scalar(out_bf[:, 64 * h:64 * h + 64], pOut[:, 64 * h:64 * h + 64],
                                               rrec[:, h:h + 1], None, ALU.mult)
                    return last
                P.add("vector", onorm)
                P.ph()
                items = [(pTR[:, ec * 128:(ec + 1) * 128], out_bf[:, ec * 128:(ec + 1) * 128])
                         for ec in range(8)]
                P.add("tensor", tr_group(items, idt[:], SNK))
                P.ph()
                def otc(e):
                    last = None
                    for ec in range(8):
                        last = e.tensor_copy(outT[:, ec * 128:(ec + 1) * 128], pTR[:, ec * 128:(ec + 1) * 128])
                    return last
                P.add("vector", otc)
                P.ph()
                for hh in range(2):
                    def omm(e, hh=hh):
                        pg = [pS0, pS1][hh]
                        for ec in range(8):
                            e.matmul(pg[:, 0:512], outT[:, ec * 128:(ec + 1) * 128],
                                     wo_sb[:, ec * 1024 + hh * 512:ec * 1024 + (hh + 1) * 512],
                                     start=(ec == 0), stop=False)
                        e.matmul(pg[:, 0:512], ones_col[:], bo_sb[:, hh * 512:(hh + 1) * 512],
                                 start=False, stop=False)
                        return fin(e, SNK)
                    P.add("tensor", omm)
                P.ph()
                def resadd(e):
                    e.tensor_add(ot_tm[:, 0:512], pS0[:, 0:512], xr_tm[:, 0:512])
                    return e.tensor_add(ot_tm[:, 512:1024], pS1[:, 0:512], xr_tm[:, 512:1024])
                P.add("vector", resadd)
                P.ph()
                P.add("sync", lambda e, qc=qc: e.dma_start(
                    out=xo[qc * 128:(qc + 1) * 128, :], in_=ot_tm[:]), dma=True)
                P.ph()


def build_swa1():
    import contextlib
    nc = bass.Bass("TRN2", target_bir_lowering=False, debug=False)
    x = nc.dram_tensor("x", [T, D], F32, kind="ExternalInput").ap()
    ins = declare_swa1_ins(nc)
    xo = nc.dram_tensor("xo", [T, D], F32, kind="ExternalOutput").ap()
    with contextlib.ExitStack() as stack:
        sc = stack.enter_context(nc.semaphore("sc"))
        sd = stack.enter_context(nc.semaphore("sd"))
        P = PH()
        with contextlib.ExitStack() as inner:
            emit_swa1(nc, P, inner, "", ins, x, xo)
        with nc.Block() as block:
            P.finalize(nc, block, sc, sd)
    return nc


def swa1_host_inputs(inputs, x_full):
    shared = dict(
        wqt2=np.ascontiguousarray(np.asarray(inputs["L1_attn_b_Wq"], np.float32).T),
        wkt2=np.ascontiguousarray(np.asarray(inputs["L1_attn_b_Wk"], np.float32).T),
        wvt2=np.ascontiguousarray(np.asarray(inputs["L1_attn_b_Wv"], np.float32).T),
        wot2=np.ascontiguousarray(np.asarray(inputs["L1_attn_b_Wo"], np.float32).T),
        bq2=np.ascontiguousarray(np.asarray(inputs["L1_attn_b_bq"], np.float32).reshape(8, 128).T),
        bk2=np.ascontiguousarray(np.asarray(inputs["L1_attn_b_bk"], np.float32).reshape(8, 128).T),
        bv2=np.ascontiguousarray(np.asarray(inputs["L1_attn_b_bv"], np.float32).reshape(1, 1024)),
        bo2=np.ascontiguousarray(np.asarray(inputs["L1_attn_b_bo"], np.float32).reshape(1, 1024)))
    return [dict(shared, x=np.ascontiguousarray(x_full[c])) for c in range(NCORES)]



MOE1_INS = ["wg", "wT", "bexp", "flg"]


def declare_moe1_ins(nc, counts_only=False):
    d = dict(wg=nc.dram_tensor("wg", [1024, 8], F32, kind="ExternalInput").ap())
    if not counts_only:
        d["wT"] = nc.dram_tensor("wT", [8192, 1024], F32, kind="ExternalInput").ap()
        d["bexp"] = nc.dram_tensor("bexp", [8, 1024], F32, kind="ExternalInput").ap()
        d["flg"] = nc.dram_tensor("flg", [8, 1], F32, kind="ExternalInput").ap()
    return d


def emit_moe1(nc, P, stack, pfx, ins, x_in, x_out, counts_only=False, fused=False,
              call=None, cnt=None):
    x = x_in
    xo = x_out
    wg = ins["wg"]
    if not counts_only:
        wT = ins["wT"]
        bexp = ins["bexp"]
        flg = ins["flg"]
    if fused:
        ccin = nc.dram_tensor(pfx + "ccin", [1, 8], F32, kind="Internal").ap()
        ccout = nc.dram_tensor(pfx + "ccout", [8, 8], F32, kind="Internal").ap()

    import contextlib
    if True:
        sb = lambda name, shape, dt=F32: stack.enter_context(nc.sbuf_tensor(pfx + name, shape, dt))
        psum = lambda name, shape, dt=F32: stack.enter_context(nc.psum_tensor(pfx + name, shape, dt))

        lnxT = sb("lnxT", [128, 8 * T], BF16)
        wg_sb = sb("wg_sb", [128, 8 * 8], BF16)
        lgT = sb("lgT", [8, T])
        lg_tm = sb("lg_tm", [128, 32 * 8])
        msk = sb("msk", [128, 32 * 8])
        g_sb = sb("g_sb", [128, 32 * 8])
        gts = sb("gts", [128, 32])
        sume = sb("sume", [128, 32])
        mx_sb = sb("mx_sb", [128, 32])
        nmx_sb = sb("nmx_sb", [128, 32])
        sc8 = sb("sc8", [128, 8])
        k8 = sb("k8", [128, 8])
        mp8 = sb("mp8", [128, 8])
        carry = sb("carry", [1, 8])
        cnt_sb = sb("cnt_sb", [1, 8])
        cst = sb("cst", [128, 1])
        zb = sb("zbq", [128, 128], BF16)
        zf = sb("zfq", [128, 128])
        ones128 = sb("ones128q", [128, 1])
        ones_col = sb("ones_colq", [1, 128])
        strif = sb("strif", [128, 128])
        idf = sb("idf", [128, 128])
        x_tm = sb("x_tm", [128, 1024])
        sq_sc = sb("sq_sc", [128, 1024], BF16)
        st_sc = sb("st_sc", [128, 8])
        lnx = sb("lnx", [128, 1024], BF16)

        pE0 = psum("pE0", [128, 1024])
        pE1 = psum("pE1", [128, 1024])
        pRT = psum("pRT", [128, 512])
        pTR = psum("pTRq", [128, 1024], BF16)

        idt = build_identity(P, nc, stack, pfx)
        SNK = make_sink(nc, P, stack, pfx)
        P.add("vector", lambda e: e.memset(cst[:], 1e-5))
        P.add("vector", lambda e: e.memset(ones128[:], 1.0))
        P.add("vector", lambda e: e.memset(ones_col[:], 1.0))
        P.add("gpsimd", lambda e: e.memset(strif[:], 1.0))
        for dc in range(8):
            P.add("gpsimd", lambda e, dc=dc: e.dma_start(
                out=wg_sb[:, dc * 8:(dc + 1) * 8], in_=wg[dc * 128:(dc + 1) * 128, :]), dma=True)
        P.ph()
        P.add("gpsimd", lambda e: e.affine_select(
            strif[:], strif[:], [[1, 128]], ALU.is_ge, 0.0, base=-1, channel_multiplier=-1))
        P.add("vector", lambda e: e.tensor_copy(idf[:], idt[:]))
        P.ph()

        for j in range(32):
            t0 = j * 128
            P.add("sync", lambda e, t0=t0: e.dma_start(out=x_tm[:], in_=x[t0:t0 + 128, :]), dma=True)
            P.ph()
            emit_ln(P, nc, x_tm, lnx, sq_sc, st_sc, cst[:, 0:1], 1)
            items = [(pTR[:, dc * 128:(dc + 1) * 128], lnx[:, dc * 128:(dc + 1) * 128])
                     for dc in range(8)]
            P.add("tensor", tr_group(items, idt[:], SNK))
            P.ph()
            def trc_fn(e, t0=t0):
                last = None
                for dc in range(8):
                    last = e.tensor_copy(lnxT[:, dc * T + t0:dc * T + t0 + 128],
                                         pTR[:, dc * 128:(dc + 1) * 128])
                return last
            P.add("vector", trc_fn)
            P.ph()

        for s in range(8):
            pairs = [(wg_sb[:, dc * 8:(dc + 1) * 8], lnxT[:, dc * T + s * 512:dc * T + (s + 1) * 512])
                     for dc in range(8)]
            P.add("tensor", mm_group(pRT[0:8, 0:512], pairs, SNK))
            P.ph()
            P.add("scalar", lambda e, s=s: e.copy(lgT[:, s * 512:(s + 1) * 512], pRT[0:8, 0:512]))
            P.ph()
        for r in range(8):
            items = [(pRT[:, j4 * 8:(j4 + 1) * 8], lgT[0:8, (r * 4 + j4) * 128:(r * 4 + j4 + 1) * 128])
                     for j4 in range(4)]
            P.add("tensor", tr_group(items, idf[0:8, 0:8], SNK))
            P.ph()
            def lgc_fn(e, r=r):
                last = None
                for j4 in range(4):
                    j = r * 4 + j4
                    last = e.tensor_copy(lg_tm[:, j * 8:(j + 1) * 8], pRT[:, j4 * 8:(j4 + 1) * 8])
                return last
            P.add("vector", lgc_fn)
            P.ph()

        def mx_fn(e):
            last = None
            for j in range(32):
                last = e.reduce_max(mx_sb[:, j:j + 1], lg_tm[:, j * 8:(j + 1) * 8], axis=AX.X)
            for j in range(32):
                last = e.reduce_max(nmx_sb[:, j:j + 1], lg_tm[:, j * 8:(j + 1) * 8], axis=AX.X, negate=True)
            return last
        P.add("vector", mx_fn)
        P.ph()
        def oh_fn(e):
            last = None
            for j in range(32):
                last = e.tensor_scalar(msk[:, j * 8:(j + 1) * 8], lg_tm[:, j * 8:(j + 1) * 8],
                                       mx_sb[:, j:j + 1], None, ALU.is_ge)
            return last
        P.add("vector", oh_fn)
        def ex_fn(e):
            last = None
            for j in range(32):
                last = e.activation(sc8[:], lg_tm[:, j * 8:(j + 1) * 8], AF.Exp,
                                    bias=nmx_sb[:, j:j + 1], accum_out=sume[:, j:j + 1])
            return last
        P.add("scalar", ex_fn)
        P.ph()
        P.add("vector", lambda e: e.reciprocal(gts[:], sume[:]))
        P.ph()
        def cnt_fn(e):
            for j in range(32):
                e.matmul(pRT[0:1, 0:8], ones128[:], msk[:, j * 8:(j + 1) * 8],
                         start=(j == 0), stop=False)
            return fin(e, SNK)
        P.add("tensor", cnt_fn)
        P.ph()
        P.add("vector", lambda e: e.tensor_copy(cnt_sb[:], pRT[0:1, 0:8]))
        P.ph()
        if counts_only:
            P.add("sync", lambda e: e.dma_start(out=cnt, in_=cnt_sb[:]), dma=True)
            P.ph()
            return

        call_sb = sb("call_sb", [8, 8])
        flg_sb = sb("flg_sb", [8, 1])
        gem = sb("gem", [8, 32 * 128], BF16)
        bexp_sb = sb("bexp_sb", [8, 1024], BF16)
        we = sb("we", [128, 2 * 8 * 1024], BF16)
        acc = sb("acc", [128, 32 * 1024], BF16)
        xr_tm = sb("xr_tm", [128, 1024])

        if fused:
            P.add("gpsimd", lambda e: e.dma_start(out=ccin, in_=cnt_sb[:]), dma=True)
            P.ph()
            P.add("gpsimd", lambda e: e.collective_compute(
                "AllGather", ALU.bypass, replica_groups=[list(range(NCORES))],
                ins=[ccin.opt()], outs=[ccout.opt()]))
            P.ph()
            P.add("gpsimd", lambda e: e.dma_start(out=call_sb[:], in_=ccout), dma=True)
        else:
            P.add("sync", lambda e: e.dma_start(out=call_sb[:], in_=call), dma=True)
        P.add("sync", lambda e: e.dma_start(out=flg_sb[:], in_=flg), dma=True)
        P.add("gpsimd", lambda e: e.dma_start(out=bexp_sb[:], in_=bexp), dma=True)
        P.ph()
        def base_fn(e):
            e.matmul(pRT[0:1, 128:136], flg_sb[:], call_sb[:], start=True, stop=False)
            return fin(e, SNK)
        P.add("tensor", base_fn)
        P.ph()
        P.add("vector", lambda e: e.tensor_copy(carry[:], pRT[0:1, 128:136]))
        P.ph()
        for j in range(32):
            def loc_fn(e, j=j):
                e.matmul(pRT[:, 0:8], strif[:], msk[:, j * 8:(j + 1) * 8], start=True, stop=False)
                e.matmul(pRT[:, 0:8], ones_col[:], carry[:], start=False, stop=False)
                e.matmul(pRT[0:1, 256:264], ones128[:], msk[:, j * 8:(j + 1) * 8], start=True, stop=False)
                return fin(e, SNK)
            P.add("tensor", loc_fn)
            P.ph()
            P.add("vector", lambda e, j=j: e.tensor_scalar(k8[:], pRT[:, 0:8], 4096.0, None, ALU.is_lt))
            P.ph()
            P.add("vector", lambda e, j=j: e.tensor_mul(mp8[:], k8[:], msk[:, j * 8:(j + 1) * 8]))
            P.ph()
            P.add("vector", lambda e, j=j: e.tensor_add(carry[:], carry[:], pRT[0:1, 256:264]))
            P.ph()
            P.add("vector", lambda e, j=j: e.tensor_scalar(
                g_sb[:, j * 8:(j + 1) * 8], mp8[:], gts[:, j:j + 1], None, ALU.mult))
            P.ph()
        for r in range(8):
            items = [(pRT[0:8, j4 * 128:(j4 + 1) * 128],
                      g_sb[:, (r * 4 + j4) * 8:(r * 4 + j4 + 1) * 8]) for j4 in range(4)]
            P.add("tensor", tr_group(items, idf[:], SNK))
            P.ph()
            def gtc_fn(e, r=r):
                last = None
                for j4 in range(4):
                    j = r * 4 + j4
                    last = e.tensor_copy(gem[:, j * 128:(j + 1) * 128], pRT[0:8, j4 * 128:(j4 + 1) * 128])
                return last
            P.add("vector", gtc_fn)
            P.ph()
        for j in range(34):
            if j < 32:
                pg = [pE0, pE1][j % 2]
                def bias_fn(e, j=j, pg=pg):
                    for hh in range(2):
                        e.matmul(pg[:, hh * 512:(hh + 1) * 512], gem[:, j * 128:(j + 1) * 128],
                                 bexp_sb[:, hh * 512:(hh + 1) * 512], start=True, stop=False)
                    return fin(e, SNK)
                P.add("tensor", bias_fn)
            if j >= 2 or (j >= 1 and True):
                jj = j - 1
                if 0 <= jj < 32:
                    pg = [pE0, pE1][jj % 2]
                    P.add("scalar", lambda e, jj=jj, pg=pg: e.copy(
                        acc[:, jj * 1024:(jj + 1) * 1024], pg[:, 0:1024]))
            P.ph()
        for ei in range(8):
            par = ei % 2
            P.add("gpsimd", lambda e, ei=ei, par=par: e.dma_start(
                out=we[:, par * 8192:(par + 1) * 8192].rearrange("p (dc f) -> p dc f", dc=8),
                in_=wT[ei * 1024:(ei + 1) * 1024, :].rearrange("(dc p) f -> p dc f", p=128)), dma=True)
            P.ph()
            for j in range(33):
                if j < 32:
                    pg = [pE0, pE1][j % 2]
                    pairs = [(lnxT[:, dc * T + j * 128:dc * T + (j + 1) * 128],
                              we[:, par * 8192 + dc * 1024:par * 8192 + dc * 1024 + 512]) for dc in range(8)]
                    pairs2 = [(lnxT[:, dc * T + j * 128:dc * T + (j + 1) * 128],
                               we[:, par * 8192 + dc * 1024 + 512:par * 8192 + (dc + 1) * 1024]) for dc in range(8)]
                    def emm(e, pairs=pairs, pairs2=pairs2, pg=pg):
                        for i, (l, rr) in enumerate(pairs):
                            e.matmul(pg[:, 0:512], l, rr, start=(i == 0), stop=False)
                        for i, (l, rr) in enumerate(pairs2):
                            e.matmul(pg[:, 512:1024], l, rr, start=(i == 0), stop=False)
                        return fin(e, SNK)
                    P.add("tensor", emm)
                if j > 0:
                    jj = j - 1
                    pg = [pE0, pE1][jj % 2]
                    P.add("vector", lambda e, jj=jj, pg=pg, ei=ei: e.scalar_tensor_tensor(
                        acc[:, jj * 1024:(jj + 1) * 1024], pg[:, 0:1024],
                        g_sb[:, jj * 8 + ei:jj * 8 + ei + 1],
                        acc[:, jj * 1024:(jj + 1) * 1024], ALU.mult, ALU.add))
                P.ph()
        for j in range(32):
            P.add("sync", lambda e, j=j: e.dma_start(
                out=xr_tm[:], in_=x[j * 128:(j + 1) * 128, :]), dma=True)
            P.ph()
            P.add("vector", lambda e, j=j: e.tensor_add(
                xr_tm[:], acc[:, j * 1024:(j + 1) * 1024], xr_tm[:]))
            P.ph()
            P.add("sync", lambda e, j=j: e.dma_start(
                out=xo[j * 128:(j + 1) * 128, :], in_=xr_tm[:]), dma=True)
            P.ph()


def build_moe1(counts_only=False):
    import contextlib
    nc = bass.Bass("TRN2", target_bir_lowering=False, debug=False)
    x = nc.dram_tensor("x", [T, D], F32, kind="ExternalInput").ap()
    ins = declare_moe1_ins(nc, counts_only)
    call = cnt = xo = None
    if counts_only:
        cnt = nc.dram_tensor("cnt", [1, 8], F32, kind="ExternalOutput").ap()
    else:
        call = nc.dram_tensor("call", [8, 8], F32, kind="ExternalInput").ap()
        xo = nc.dram_tensor("xo", [T, D], F32, kind="ExternalOutput").ap()
    with contextlib.ExitStack() as stack:
        sc = stack.enter_context(nc.semaphore("sc"))
        sd = stack.enter_context(nc.semaphore("sd"))
        P = PH()
        with contextlib.ExitStack() as inner:
            emit_moe1(nc, P, inner, "", ins, x, xo, counts_only=counts_only, call=call, cnt=cnt)
        with nc.Block() as block:
            P.finalize(nc, block, sc, sd)
    return nc


def moe1_host_inputs(inputs, x_full, counts_all=None):
    wg = np.ascontiguousarray(np.asarray(inputs["L1_ffn_wg"], np.float32))
    maps = []
    base = dict(wg=wg)
    if counts_all is not None:
        wT = np.ascontiguousarray(np.transpose(np.asarray(inputs["L1_ffn_W"], np.float32), (0, 2, 1))
                                  .reshape(8192, 1024))
        bexp = np.ascontiguousarray(np.asarray(inputs["L1_ffn_b"], np.float32))
        base = dict(base, wT=wT, bexp=bexp, call=np.ascontiguousarray(counts_all.astype(np.float32)))
    for c in range(NCORES):
        m = dict(base, x=np.ascontiguousarray(x_full[c]))
        if counts_all is not None:
            f = np.zeros((8, 1), np.float32); f[:c] = 1.0
            m["flg"] = f
        maps.append(m)
    return maps




MOE0F_INS = ["w1", "w2", "ewf", "fl0"]


def declare_moe0f_ins(nc):
    return dict(
        w1=nc.dram_tensor("w1", [1024, 8192], F32, kind="ExternalInput").ap(),
        w2=nc.dram_tensor("w2", [8192, 1024], F32, kind="ExternalInput").ap(),
        ewf=nc.dram_tensor("ewf", [128, 64], F32, kind="ExternalInput").ap(),
        fl0=nc.dram_tensor("fl0", [128, 1], F32, kind="ExternalInput").ap())


def emit_moe0_full(nc, P, stack, pfx, ins, x_io):
    w1, w2, ewf, fl0 = (ins[k] for k in MOE0F_INS)
    sb = lambda name, shape, dt=F32: stack.enter_context(nc.sbuf_tensor(pfx + name, shape, dt))
    psum = lambda name, shape, dt=F32: stack.enter_context(nc.psum_tensor(pfx + name, shape, dt))

    lnxT = sb("lnxT", [128, 8 * T], BF16)
    w1g = [sb("w1g0", [128, 8 * 1024], BF16), sb("w1g1", [128, 8 * 1024], BF16)]
    w2g = [sb("w2g0", [128, 8 * 1024], BF16), sb("w2g1", [128, 8 * 1024], BF16)]
    h_bf = sb("h_bf", [128, 8 * 512], BF16)
    ystage = sb("ystage", [128, 2 * 4096])
    ew_sb = sb("ew_sb", [128, 64])
    s_sb = sb("s_sb", [128, 32])
    x_tm = sb("x_tm", [128, 1024])
    sq_sc = sb("sq_sc", [128, 1024], BF16)
    st_sc = sb("st_sc", [128, 8])
    lnx = sb("lnx", [128, 1024], BF16)
    cst = sb("cst", [128, 1])
    zb = sb("zb", [128, 128], BF16)
    pH = psum("pH", [128, 1024])
    pY0 = psum("pY0", [128, 1024])
    pY1 = psum("pY1", [128, 1024])
    pTR = psum("pTR", [128, 1024], BF16)

    idt = build_identity(P, nc, stack, pfx)
    SNK = make_sink(nc, P, stack, pfx)
    P.add("sync", lambda e: e.dma_start(out=ew_sb[:], in_=ewf), dma=True)
    P.add("vector", lambda e: e.memset(cst[:], 1e-5))
    P.add("gpsimd", lambda e: e.dma_start(
        out=w1g[0][:].rearrange("p (dc i) -> p dc i", dc=8),
        in_=w1[:, 0:1024].rearrange("(dc p) i -> p dc i", p=128)), dma=True)
    P.add("gpsimd", lambda e: e.dma_start(
        out=w2g[0][:].rearrange("p (ic f) -> p ic f", ic=8),
        in_=w2[0:1024, :].rearrange("(ic p) f -> p ic f", p=128)), dma=True)
    fl_sb = sb("fl_sb", [128, 1])
    P.add("sync", lambda e: e.dma_start(out=fl_sb[:], in_=fl0), dma=True)
    P.ph()
    P.add("vector", lambda e: e.tensor_add(s_sb[:], ew_sb[:, 0:32], ew_sb[:, 32:64]))
    P.ph()
    P.add("vector", lambda e: e.tensor_scalar(s_sb[:], s_sb[:], fl_sb[:, 0:1], None, ALU.mult))
    P.ph()
    for j in range(32):
        t0 = j * 128
        P.add("sync", lambda e, t0=t0: e.dma_start(out=x_tm[:], in_=x_io[t0:t0 + 128, :]), dma=True)
        P.ph()
        emit_ln(P, nc, x_tm, lnx, sq_sc, st_sc, cst[:, 0:1], 1)
        items = [(pTR[:, dc * 128:(dc + 1) * 128], lnx[:, dc * 128:(dc + 1) * 128])
                 for dc in range(8)]
        P.add("tensor", tr_group(items, idt[:], SNK))
        P.ph()
        def trc_fn(e, t0=t0):
            last = None
            for dc in range(8):
                last = e.tensor_copy(lnxT[:, dc * T + t0:dc * T + t0 + 128],
                                     pTR[:, dc * 128:(dc + 1) * 128])
            return last
        P.add("vector", trc_fn)
        P.ph()
    pYs = [pY0, pY1]
    for g in range(8):
        gb = g % 2
        if g < 7:
            P.add("gpsimd", lambda e, g=g: e.dma_start(
                out=w1g[(g + 1) % 2][:].rearrange("p (dc i) -> p dc i", dc=8),
                in_=w1[:, (g + 1) * 1024:(g + 2) * 1024].rearrange("(dc p) i -> p dc i", p=128)), dma=True)
            P.add("gpsimd", lambda e, g=g: e.dma_start(
                out=w2g[(g + 1) % 2][:].rearrange("p (ic f) -> p ic f", ic=8),
                in_=w2[(g + 1) * 1024:(g + 2) * 1024, :].rearrange("(ic p) f -> p ic f", p=128)), dma=True)
        for tb in range(8):
            tpar = tb % 2
            for ic in range(9):
                if ic < 8:
                    pairs = [(w1g[gb][:, dc * 1024 + ic * 128:dc * 1024 + (ic + 1) * 128],
                              lnxT[:, dc * T + tb * 512:dc * T + (tb + 1) * 512]) for dc in range(8)]
                    P.add("tensor", mm_group(pH[:, (ic % 2) * 512:(ic % 2 + 1) * 512], pairs, SNK))
                if ic > 0:
                    P.add("scalar", lambda e, ic=ic: e.activation(
                        h_bf[:, (ic - 1) * 512:ic * 512], pH[:, ((ic - 1) % 2) * 512:((ic - 1) % 2 + 1) * 512],
                        AF.Gelu_apprx_tanh))
                P.ph()
            for c in range(5):
                if c < 4:
                    py = pYs[c % 2]
                    for hh in range(2):
                        pairs = [(h_bf[:, ic2 * 512 + c * 128:ic2 * 512 + (c + 1) * 128],
                                  w2g[gb][:, ic2 * 1024 + hh * 512:ic2 * 1024 + (hh + 1) * 512])
                                 for ic2 in range(8)]
                        P.add("tensor", mm_group(py[:, hh * 512:(hh + 1) * 512], pairs, SNK))
                if c > 0:
                    py = pYs[(c - 1) % 2]
                    P.add("vector", lambda e, c=c, py=py, tb=tb, tpar=tpar: e.tensor_scalar(
                        ystage[:, tpar * 4096 + (c - 1) * 1024:tpar * 4096 + c * 1024],
                        py[:, 0:1024], s_sb[:, tb * 4 + (c - 1):tb * 4 + c], None, ALU.mult))
                P.ph()
            P.add("gpsimd", lambda e, tb=tb, tpar=tpar: e.dma_start(
                out=x_io[tb * 512:(tb + 1) * 512, :].rearrange("(c p) d -> p c d", p=128),
                in_=ystage[:, tpar * 4096:(tpar + 1) * 4096].rearrange("p (c d) -> p c d", c=4),
                accum_op=ALU.add), dma=True)
            P.ph()


def moe0f_host_inputs(inputs):
    ew = np.asarray(inputs["L0_ffn_expert_weights"], np.float32)
    w1 = np.ascontiguousarray(np.asarray(inputs["L0_ffn_w1"], np.float32))
    w2 = np.ascontiguousarray(np.asarray(inputs["L0_ffn_w2"], np.float32))
    shared = dict(w1=w1, w2=w2)
    out = []
    for c in range(NCORES):
        rows = slice(c * 512, (c + 1) * 512)
        ew_c = ew[rows].reshape(4, 128, 2).transpose(1, 2, 0).reshape(128, 8)
        fl = np.full((128, 1), 1.0 if c == 0 else 0.0, np.float32)
        out.append(dict(shared, ews=np.ascontiguousarray(ew_c), fl0=fl))
    return out



def build_fused(upto=8):
    import contextlib
    nc = bass.Bass("TRN2", target_bir_lowering=False, debug=False)
    x_ext = nc.dram_tensor("x", [T, D], F32, kind="ExternalInput").ap()
    xo_ext = nc.dram_tensor("xo", [T, D], F32, kind="ExternalOutput").ap()
    xcur = nc.dram_tensor("xcur", [T, D], F32, kind="Internal").ap()
    ins_m = declare_mamba_ins(nc)
    ins_f0 = declare_moe0r_ins(nc)
    ins_a0 = declare_attn0_ins(nc)
    ins_mh = declare_mha1_ins(nc)
    ins_q1 = declare_moe1_ins(nc)
    ins_sw = declare_swa1_ins(nc)

    with contextlib.ExitStack() as stack:
        sc = stack.enter_context(nc.semaphore("sc"))
        sd = stack.enter_context(nc.semaphore("sd"))
        P = PH()
        steps = []
        steps.append(lambda s: emit_mamba(nc, P, s, "m_", ins_m, x_ext, xcur))
        steps.append(lambda s: emit_moe0_rs(nc, P, s, "fa_", ins_f0, xcur))
        steps.append(lambda s: emit_attn0(nc, P, s, "a0_", ins_a0, xcur, xcur))
        steps.append(lambda s: emit_moe0_rs(nc, P, s, "fb_", ins_f0, xcur))
        steps.append(lambda s: emit_mha1(nc, P, s, "mh_", ins_mh, xcur, xcur))
        steps.append(lambda s: emit_moe1(nc, P, s, "qa_", ins_q1, xcur, xcur, fused=True))
        steps.append(lambda s: emit_swa1(nc, P, s, "sw_", ins_sw, xcur, xcur))
        steps.append(lambda s: emit_moe1(nc, P, s, "qb_", ins_q1, xcur,
                                         xo_ext if upto >= 8 else xcur, fused=True))
        nc._section_marks = []
        for i, st in enumerate(steps[:upto]):
            with contextlib.ExitStack() as s:
                st(s)
            nc._section_marks.append((i, len(P.phases)))
        nc._PH = P
        if upto < 8:
            for j in range(8):
                P.add("sync", lambda e, j=j: e.dma_start(
                    out=xo_ext[j * 512:(j + 1) * 512, :], in_=xcur[j * 512:(j + 1) * 512, :]), dma=True)
            P.ph()
        with nc.Block() as block:
            P.finalize(nc, block, sc, sd)
    return nc


def fused_host_inputs(inputs):
    x = np.ascontiguousarray(np.asarray(inputs["L0_attn_a_x"], np.float32))
    mam = mamba_host_inputs(dict(inputs, L0_attn_a_x=x))
    f0 = moe0f_host_inputs(inputs)
    a0 = attn0_host_inputs(inputs, x)
    mh = mha1_host_inputs(inputs, x)
    sw = swa1_host_inputs(inputs, x)
    q1 = moe1_host_inputs(inputs, x, counts_all=np.zeros((8, 8), np.float32))
    maps = []
    for c in range(NCORES):
        m = {}
        m.update({k: v for k, v in mam[c].items()})
        m.update(f0[c])
        m.update({k: v for k, v in a0[c].items() if k != "x"})
        m.update({k: v for k, v in mh[c].items() if k != "x"})
        m.update({k: v for k, v in sw[c].items() if k != "x"})
        m.update({k: v for k, v in q1[c].items() if k not in ("x", "call")})
        maps.append(m)
    return maps


def kernel(**inputs):
    nc = build_fused(upto=8)
    maps = fused_host_inputs(inputs)
    res = run_bass_kernel_spmd(nc, maps, core_ids=list(range(NCORES)))
    return np.stack([res.results[c]["xo"] for c in range(NCORES)])



MOE0R_INS = ["w1", "w2", "ews", "fl0"]


def declare_moe0r_ins(nc):
    return dict(
        w1=nc.dram_tensor("w1", [1024, 8192], F32, kind="ExternalInput").ap(),
        w2=nc.dram_tensor("w2", [8192, 1024], F32, kind="ExternalInput").ap(),
        ews=nc.dram_tensor("ews", [128, 8], F32, kind="ExternalInput").ap(),
        fl0=nc.dram_tensor("fl0", [128, 1], F32, kind="ExternalInput").ap())


def emit_moe0_rs(nc, P, stack, pfx, ins, x_io):
    w1, w2, ews, fl0 = (ins[k] for k in MOE0R_INS)
    sb = lambda name, shape, dt=F32: stack.enter_context(nc.sbuf_tensor(pfx + name, shape, dt))
    psum = lambda name, shape, dt=F32: stack.enter_context(nc.psum_tensor(pfx + name, shape, dt))
    rsi = nc.dram_tensor(pfx + "rsi", [T, D], BF16, kind="Internal").ap()
    rso = nc.dram_tensor(pfx + "rso", [512, D], BF16, kind="Internal").ap()
    agi = nc.dram_tensor(pfx + "agi", [512, D], BF16, kind="Internal").ap()
    ago = nc.dram_tensor(pfx + "ago", [T, D], BF16, kind="Internal").ap()
    TB = 512
    TC = 4

    w1g = [sb("w1g0", [128, 8 * 1024], BF16), sb("w1g1", [128, 8 * 1024], BF16)]
    w2g = [sb("w2g0", [128, 8 * 1024], BF16), sb("w2g1", [128, 8 * 1024], BF16)]
    lnxT = sb("lnxTs", [128, 8 * TB], BF16)
    h_bf = sb("h_bf", [128, 8 * TB], BF16)
    acc = sb("acc", [128, TC * 1024])
    ystage = sb("ystage", [128, TC * 1024], BF16)
    ew_sb = sb("ew_sb", [128, 8])
    s_sb = sb("s_sb", [128, TC])
    x_tm = sb("x_tm", [128, 1024])
    sq_sc = sb("sq_sc", [128, 1024], BF16)
    st_sc = sb("st_sc", [128, 8])
    lnx = sb("lnx", [128, 1024], BF16)
    fl_sb = sb("fl_sb", [128, 1])
    cst = sb("cst", [128, 1])
    fstage = sb("fstage", [128, 4096])
    ysb = sb("ysb", [128, 4096], BF16)
    pH = psum("pH", [128, 1024])
    pY0 = psum("pY0", [128, 1024])
    pY1 = psum("pY1", [128, 1024])
    pTR = psum("pTR", [128, 1024], BF16)

    idt = build_identity(P, nc, stack, pfx)
    SNK = make_sink(nc, P, stack, pfx)
    P.add("sync", lambda e: e.dma_start(out=ew_sb[:], in_=ews), dma=True)
    P.add("sync", lambda e: e.dma_start(out=fl_sb[:], in_=fl0), dma=True)
    P.add("vector", lambda e: e.memset(cst[:], 1e-5))
    P.add("gpsimd", lambda e: e.memset(acc[:], 0.0))
    P.add("gpsimd", lambda e: e.dma_start(
        out=w1g[0][:].rearrange("p (dc i) -> p dc i", dc=8),
        in_=w1[:, 0:1024].rearrange("(dc p) i -> p dc i", p=128)), dma=True)
    P.add("gpsimd", lambda e: e.dma_start(
        out=w2g[0][:].rearrange("p (ic f) -> p ic f", ic=8),
        in_=w2[0:1024, :].rearrange("(ic p) f -> p ic f", p=128)), dma=True)
    P.ph()
    P.add("vector", lambda e: e.tensor_add(s_sb[:], ew_sb[:, 0:TC], ew_sb[:, TC:2 * TC]))
    P.ph()
    for j in range(32):
        t0 = j * 128
        P.add("sync", lambda e, t0=t0: e.dma_start(out=x_tm[:], in_=x_io[t0:t0 + 128, :]), dma=True)
        P.ph()
        emit_ln(P, nc, x_tm, lnx, sq_sc, st_sc, cst[:, 0:1], 1)
        P.add("vector", lambda e: e.tensor_scalar(lnx[:], lnx[:], fl_sb[:, 0:1], None, ALU.mult))
        P.ph()
        P.add("sync", lambda e, t0=t0: e.dma_start(out=rsi[t0:t0 + 128, :], in_=lnx[:]), dma=True)
        P.ph()
    P.add("gpsimd", lambda e: e.collective_compute(
        "ReduceScatter", ALU.add, replica_groups=[list(range(NCORES))],
        ins=[rsi.opt()], outs=[rso.opt()]))
    P.ph()
    for c in range(TC):
        P.add("sync", lambda e, c=c: e.dma_start(out=lnx[:], in_=rso[c * 128:(c + 1) * 128, :]), dma=True)
        P.ph()
        items = [(pTR[:, dc * 128:(dc + 1) * 128], lnx[:, dc * 128:(dc + 1) * 128])
                 for dc in range(8)]
        P.add("tensor", tr_group(items, idt[:], SNK))
        P.ph()
        def trc_fn(e, c=c):
            last = None
            for dc in range(8):
                last = e.tensor_copy(lnxT[:, dc * TB + c * 128:dc * TB + (c + 1) * 128],
                                     pTR[:, dc * 128:(dc + 1) * 128])
            return last
        P.add("vector", trc_fn)
        P.ph()
    pYs = [pY0, pY1]
    for g in range(8):
        gb = g % 2
        if g < 7:
            P.add("gpsimd", lambda e, g=g: e.dma_start(
                out=w1g[(g + 1) % 2][:].rearrange("p (dc i) -> p dc i", dc=8),
                in_=w1[:, (g + 1) * 1024:(g + 2) * 1024].rearrange("(dc p) i -> p dc i", p=128)), dma=True)
            P.add("gpsimd", lambda e, g=g: e.dma_start(
                out=w2g[(g + 1) % 2][:].rearrange("p (ic f) -> p ic f", ic=8),
                in_=w2[(g + 1) * 1024:(g + 2) * 1024, :].rearrange("(ic p) f -> p ic f", p=128)), dma=True)
        for ic in range(9):
            if ic < 8:
                pairs = [(w1g[gb][:, dc * 1024 + ic * 128:dc * 1024 + (ic + 1) * 128],
                          lnxT[:, dc * TB:(dc + 1) * TB]) for dc in range(8)]
                P.add("tensor", mm_group(pH[:, (ic % 2) * 512:(ic % 2 + 1) * 512], pairs, SNK))
            if ic > 0:
                P.add("scalar", lambda e, ic=ic: e.activation(
                    h_bf[:, (ic - 1) * TB:ic * TB], pH[:, ((ic - 1) % 2) * 512:((ic - 1) % 2 + 1) * 512],
                    AF.Gelu_apprx_tanh))
            P.ph()
        for c in range(TC + 1):
            if c < TC:
                py = pYs[c % 2]
                for hh in range(2):
                    pairs = [(h_bf[:, ic2 * TB + c * 128:ic2 * TB + (c + 1) * 128],
                              w2g[gb][:, ic2 * 1024 + hh * 512:ic2 * 1024 + (hh + 1) * 512])
                             for ic2 in range(8)]
                    P.add("tensor", mm_group(py[:, hh * 512:(hh + 1) * 512], pairs, SNK))
            if c > 0:
                py = pYs[(c - 1) % 2]
                P.add("vector", lambda e, c=c, py=py: e.tensor_add(
                    acc[:, (c - 1) * 1024:c * 1024], py[:, 0:1024], acc[:, (c - 1) * 1024:c * 1024]))
            P.ph()
    def sc_fn(e):
        last = None
        for c in range(TC):
            last = e.tensor_scalar(ystage[:, c * 1024:(c + 1) * 1024], acc[:, c * 1024:(c + 1) * 1024],
                                   s_sb[:, c:c + 1], None, ALU.mult)
        return last
    P.add("vector", sc_fn)
    P.ph()
    P.add("sync", lambda e: e.dma_start(
        out=agi.rearrange("(c p) d -> p c d", p=128),
        in_=ystage[:].rearrange("p (c d) -> p c d", c=TC)), dma=True)
    P.ph()
    P.add("gpsimd", lambda e: e.collective_compute(
        "AllGather", ALU.bypass, replica_groups=[list(range(NCORES))],
        ins=[agi.opt()], outs=[ago.opt()]))
    P.ph()
    for b in range(8):
        t0 = b * 512
        P.add("sync", lambda e, t0=t0: e.dma_start(
            out=ysb[:].rearrange("p (c d) -> p c d", c=4),
            in_=ago[t0:t0 + 512, :].rearrange("(c p) d -> p c d", p=128)), dma=True)
        P.ph()
        P.add("vector", lambda e: e.tensor_scalar(fstage[:], ysb[:], fl_sb[:, 0:1], None, ALU.mult))
        P.ph()
        P.add("gpsimd", lambda e, t0=t0: e.dma_start(
            out=x_io[t0:t0 + 512, :].rearrange("(c p) d -> p c d", p=128),
            in_=fstage[:].rearrange("p (c d) -> p c d", c=4),
            accum_op=ALU.add), dma=True)
        P.ph()
